# version 21
# baseline (speedup 1.0000x reference)
import numpy as np
import ml_dtypes
import concourse.bass as bass
import concourse.mybir as mybir
import concourse.bass_utils as _bass_utils
from concourse.bass_utils import run_bass_kernel_spmd
from contextlib import ExitStack

if not getattr(_bass_utils, "_ldw_opt_patched", False):
    _orig_run_command = _bass_utils.run_command

    def _run_command_ldw(cmd, *a, **kw):
        if isinstance(cmd, (list, tuple)):
            cmd = ["--enable-ldw-opt=true" if c == "--enable-ldw-opt=false"
                   else c for c in cmd]
        return _orig_run_command(cmd, *a, **kw)

    _bass_utils.run_command = _run_command_ldw
    _bass_utils._ldw_opt_patched = True

F32 = mybir.dt.float32
BF16 = mybir.dt.bfloat16
AF = mybir.ActivationFunctionType
ALU = mybir.AluOpType
AX = mybir.AxisListType

D = 1024
NH = 8
DH = 128
I_FFN = 2816
NCORES = 8


class Chan:
    def __init__(self, stack, nc, name):
        self.h = stack.enter_context(nc.semaphore(name))
        self.n = 0


class Ring:
    def __init__(self, nslots):
        self.nslots = nslots
        self.marks = []

    def guard(self):
        if len(self.marks) < self.nslots:
            return None
        return self.marks[-self.nslots]

    def done(self, chan):
        self.marks.append((chan, chan.n))

    def done_at(self, chan, val):
        self.marks.append((chan, val))


class Builder:
    def __init__(self, T=2048, NSEQ=4):
        self.T = T
        self.NSEQ = NSEQ
        self.NTC = T // 128
        self.NQB = T // 512
        self.steps = []
        self.stack = ExitStack()
        self.nc = bass.Bass("TRN2", target_bir_lowering=False)
        self._build()

    def em(self, key, fn):
        self.steps.append((key, fn))

    def wait(self, key, ch, val=None):
        v = ch.n if val is None else val
        if v <= 0:
            return
        self.em(key, lambda E, s=ch.h, vv=v: E.wait_ge(s, vv))

    def waitm(self, key, mark):
        if mark is None:
            return
        ch, v = mark
        self.wait(key, ch, v)

    def op(self, key, fn, ch=None):
        if ch is None:
            self.em(key, fn)
        else:
            ch.n += 1
            self.em(key, lambda E, f=fn, s=ch.h: f(E).then_inc(s, 1))

    def drain(self, key):
        self.em(key, lambda E: E.drain())

    def dma(self, key, out_ap, in_ap, ch):
        ch.n += 16
        self.em(key, lambda E, o=out_ap, i=in_ap, s=ch.h:
                E.dma_start(out=o, in_=i).then_inc(s, 16))

    def dma_pool(self, key, out_ap, in_ap):
        ch = self.dpool[self.dpool_i % len(self.dpool)]
        self.dpool_i += 1
        if ch.n > 0:
            self.wait(key, ch)
        self.dma(key, out_ap, in_ap, ch)
        return (ch, ch.n)

    def sbuf(self, name, shape, dt=F32, stack=None):
        return (stack or self.stack).enter_context(
            self.nc.sbuf_tensor(name, shape, dt))

    def psum(self, name, shape, dt=F32):
        return self.stack.enter_context(self.nc.psum_tensor(name, shape, dt))

    def chan(self, name):
        return Chan(self.stack, self.nc, name)

    def scope_barrier(self):
        for k in ('Y', 'G'):
            self.wait(k, self.chT)
            self.wait(k, self.chV)
            self.wait(k, self.chS)
            self.wait(k, self.dO)
            self.wait(k, self.dY)
            self.wait(k, self.dG)
            self.wait(k, self.dS[0])
            self.wait(k, self.dS[1])
            for p in self.dpool:
                self.wait(k, p)

    def _build(self):
        nc, T, NSEQ, NTC, NQB = self.nc, self.T, self.NSEQ, self.NTC, self.NQB
        dram = lambda n, s, d=F32, k="ExternalInput": nc.dram_tensor(n, s, d, kind=k).ap()
        self.x_in = dram("x", [NSEQ, T, D])
        self.wq_d = dram("wq", [NH, D, DH])
        self.wk_d = dram("wk", [NH, D, DH])
        self.wv_d = dram("wv", [NH, D, DH])
        self.wo_d = dram("wo", [D, D])
        self.wg_d = dram("w_gate", [I_FFN, D])
        self.wu_d = dram("w_up", [I_FFN, D])
        self.wd_d = dram("w_down", [D, I_FFN])
        self.wm_d = dram("wmod", [D, 6 * D])
        self.c_d = dram("cvec", [NSEQ, D])
        self.cos_d = dram("cos_t", [128, T])
        self.sin_d = dram("sin_t", [128, T])
        self.psw_d = dram("pswap", [128, 128])
        self.diag_d = dram("diag01", [128, 512])
        self.id_d = dram("ident", [128, 128])
        self.out_d = dram("out", [NSEQ, T, D], F32, "ExternalOutput")
        scr = lambda n, s, d: nc.dram_tensor(n, s, d, kind="Internal").ap()
        self.mod_scr = scr("mod_scr", [NSEQ, 6 * D], F32)
        self.xmid_scr = scr("xmid_scr", [NSEQ, T, D], F32)
        self.wgT_scr = scr("wgT_scr", [D, I_FFN], BF16)
        self.wuT_scr = scr("wuT_scr", [D, I_FFN], BF16)
        self.wdT_scr = scr("wdT_scr", [I_FFN, D], BF16)

        self.chT = self.chan("chT")
        self.chV = self.chan("chV")
        self.chS = self.chan("chS")
        self.dY = self.chan("dY")
        self.dG = self.chan("dG")
        self.dO = self.chan("dO")
        self.dX = [self.chan("dX0"), self.chan("dX1")]
        self.dR = [self.chan("dR0"), self.chan("dR1")]
        self.dS = [self.chan("dS0"), self.chan("dS1")]
        self.dpool = [self.chan(f"dp{i}") for i in range(6)]
        self.dpool_i = 0


        self.identb = self.sbuf("identb", [128, 128], BF16)
        self.ones_c = self.sbuf("ones_c", [128, 1], BF16)
        self.ones2f = self.sbuf("ones2f", [2, 128], F32)
        self.rhs2z = self.sbuf("rhs2z", [2, D], F32)
        self.CA1 = self.sbuf("CA1", [128, D], F32)
        self.SAt = self.sbuf("SAt", [128, D], F32)
        self.GAt = self.sbuf("GAt", [128, D], F32)
        self.bns = self.sbuf("bns", [128, 12], F32)
        self.mv2 = self.sbuf("mv2", [128, 2], F32)
        self.rstd = self.sbuf("rstd", [128, 1], F32)
        self.lnv = self.sbuf("lnv", [128, 1], F32)
        self.cpool = self.sbuf("cpool", [128, 4], F32)
        for i, v in enumerate([1e-5, -0.5, -1.0, 2.0]):
            self.nc.const_aps.register(F32, v, self.cpool[:, i:i + 1])
        self.xt = [self.sbuf("xt0", [128, D]), self.sbuf("xt1", [128, D])]
        self.xw = [self.sbuf("xw0", [128, D]), self.sbuf("xw1", [128, D])]
        self.tmp_t = self.sbuf("tmp_t", [128, D])
        self.pA = self.psum("pA", [128, 512])
        self.pB = self.psum("pB", [128, 512])
        self.pC = self.psum("pC", [128, 512])
        self.pD = self.psum("pD", [128, 512])
        self.pEF = self.psum("pEF", [128, 512])
        self.pG = self.psum("pG", [128, 1024])
        self.pH = self.psum("pH", [128, 8, 128], BF16)
        self.r_xt = Ring(2)
        self.r_xw = Ring(2)
        self.r_pH = Ring(1)

        self._init_consts()
        self.scope_barrier()
        self._emit_mod()
        self.scope_barrier()
        with ExitStack() as attn_scope:
            self._alloc_attn(attn_scope)
            self._init_tables()
            self._alloc_qd(attn_scope, "_a", nsets=1)
            self._wp_store_chans = []
            self._wp_feed = self._wprep_units(
                [(self.wg_d, self.wgT_scr, 22, 1024),
                 (self.wu_d, self.wuT_scr, 22, 1024),
                 (self.wd_d, self.wdT_scr, 8, I_FFN)],
                attn_scope)
            for s in range(NSEQ):
                self._emit_attention(s)
            for _ in self._wp_feed:
                pass
            for ch in self._wp_store_chans:
                self.wait('Y', ch)
                self.wait('G', ch)

        self.scope_barrier()
        with ExitStack() as ffn_scope:
            self._alloc_qd(ffn_scope, "_f")
            self._alloc_ffn(ffn_scope)
            self.wait('Y', self.dO)
            self.wait('Y', self.dS[0])
            self.wait('Y', self.dS[1])
            wmk = [self.dma_pool('Y', self.wgT[:], self.wgT_scr.rearrange("(c p) i -> p c i", p=128)),
                   self.dma_pool('Y', self.wuT[:], self.wuT_scr.rearrange("(c p) i -> p c i", p=128)),
                   self.dma_pool('Y', self.wdT[:], self.wdT_scr.rearrange("(c p) i -> p c i", p=128))]
            self.ffn_wmk = wmk
            for s in range(NSEQ):
                self._emit_ffn(s)

        self.wait('Y', self.dO)
        self.wait('Y', self.dS[0])
        self.wait('Y', self.dS[1])

        keymap = {'Y': 'sync', 'G': 'gpsimd', 'V': 'vector', 'S': 'scalar', 'T': 'tensor'}
        with self.stack:
            with nc.Block() as block:
                for key, attr in keymap.items():
                    fns = [f for k, f in self.steps if k == key]
                    getattr(block, attr)(lambda E, fl=tuple(fns): [f(E) for f in fl])

    def _alloc_qd(self, sc, tag, nsets=2):
        self.qds = []
        for i in range(nsets):
            s = {}
            s['xn'] = self.sbuf(f"qd_xn{i}{tag}", [128, 512], BF16, sc)
            s['A'] = self.sbuf(f"qd_A{i}{tag}", [128, 512], BF16, sc)
            s['q'] = self.sbuf(f"qd_q{i}{tag}", [128, 512], BF16, sc)
            s['q2'] = self.sbuf(f"qd_q2{i}{tag}", [128, 512], BF16, sc)
            s['g'] = self.sbuf(f"qd_g{i}{tag}", [128, 512], BF16, sc)
            s['g2'] = self.sbuf(f"qd_g2{i}{tag}", [128, 512], BF16, sc)
            s['ss'] = self.sbuf(f"qd_ss{i}{tag}", [128, 512], BF16, sc)
            s['t5'] = self.sbuf(f"qd_t5{i}{tag}", [128, 512], BF16, sc)
            s['bm'] = self.sbuf(f"qd_bm{i}{tag}", [128, 32], F32, sc)
            s['scb'] = self.sbuf(f"qd_scb{i}{tag}", [128, 32], BF16, sc)
            s['rs'] = self.sbuf(f"qd_rs{i}{tag}", [128, 32], F32, sc)
            self.qds.append(s)

    def _init_consts(self):
        idf = self.xw[1]
        mk = self.dma_pool('Y', idf[:, 0:128], self.id_d)
        self.waitm('V', mk)
        self.op('V', lambda E: E.tensor_copy(self.identb[:], idf[:, 0:128]))
        self.op('V', lambda E: E.memset(self.ones_c[:], 1.0))
        self.op('V', lambda E: E.memset(self.ones2f[:], 1.0))
        self.op('V', lambda E: E.memset(self.rhs2z[:], 0.0))
        for i, v in enumerate([1e-5, -0.5, -1.0, 2.0]):
            self.op('V', lambda E, ii=i, vv=v: E.memset(self.cpool[:, ii:ii + 1], vv))
        self.drain('V')
        self.op('V', lambda E: E.engine_nop(), self.chV)
        self.r_xt.done(self.chV)
        self.r_xt.done(self.chV)
        self.r_xw.done(self.chV)
        self.r_xw.done(self.chV)

    def _init_tables(self):
        self.dma('G', self.cosb[:], self.cos_d, self.dG)
        self.dma('G', self.sinb[:], self.sin_d, self.dG)
        self.dma('G', self.pswb[:], self.psw_d, self.dG)
        self.wait('V', self.dG)
        self.wait('T', self.dG)
        self.op('V', lambda E: E.engine_nop(), self.chV)

    def _emit_mod(self):
        nc, NSEQ = self.nc, self.NSEQ
        chT, chV, chS, dY, dG, dO = self.chT, self.chV, self.chS, self.dY, self.dG, self.dO
        with ExitStack() as sc:
            cs_f = self.sbuf("cs_f", [NSEQ, 1024], F32, sc)
            cs_b = self.sbuf("cs_b", [NSEQ, 1024], BF16, sc)
            scT = self.sbuf("scT", [128, 8, NSEQ], BF16, sc)
            wmb = [self.sbuf("wmb0", [128, 8, 512], BF16, sc),
                   self.sbuf("wmb1", [128, 8, 512], BF16, sc)]
            modp = self.sbuf("modp", [NSEQ, 512], F32, sc)
            dGm = [self.chan("dGm0"), self.chan("dGm1")]
            ca = self.sbuf("cl_a", [NSEQ, 512], F32, sc)
            cb = self.sbuf("cl_b", [NSEQ, 512], F32, sc)
            cm = self.sbuf("cl_m", [NSEQ, 512], F32, sc)
            cd = self.sbuf("cl_d", [NSEQ, 512], F32, sc)

            mk = self.dma_pool('Y', cs_f[:], self.c_d)
            self.waitm('S', mk)
            self.op('S', lambda E: E.activation(cs_f[:], cs_f[:], AF.Silu), chS)
            self.wait('V', chS)
            self.op('V', lambda E: E.tensor_copy(cs_b[:], cs_f[:]), chV)
            self.wait('T', chV)
            for c in range(8):
                self.op('T', lambda E, cc=c: E.transpose(
                    self.pH[:, cc, 0:NSEQ], cs_b[0:NSEQ, cc * 128:(cc + 1) * 128],
                    self.identb[0:NSEQ, 0:NSEQ]), chT if c == 7 else None)
            self.wait('V', chT)
            self.op('V', lambda E: E.tensor_copy(scT[:], self.pH[:, :, 0:NSEQ]), chV)
            self.r_pH.done(self.chV)
            scT_ready = chV.n
            mm_marks = {}
            for nb in range(12):
                buf = wmb[nb % 2]
                if nb >= 2:
                    self.wait('G', chT, mm_marks[nb - 2])
                self.dma('G', buf[:],
                         self.wm_d[:, nb * 512:(nb + 1) * 512]
                         .rearrange("(c p) n -> p c n", p=128), dGm[nb % 2])
                self.wait('T', dGm[nb % 2])
                if nb == 0:
                    self.wait('T', chV, scT_ready)
                else:
                    self.wait('T', chV)
                for c in range(8):
                    self.op('T', lambda E, cc=c, b=buf: E.matmul(
                        self.pA[0:NSEQ, :], scT[:, cc, :], b[:, cc, :],
                        start=(cc == 0), stop=(cc == 7)), chT if c == 7 else None)
                mm_marks[nb] = chT.n
                self.wait('V', chT)
                self.op('V', lambda E: E.tensor_copy(modp[:], self.pA[0:NSEQ, :]))
                self.drain('V')
                if nb in (4, 5, 10, 11):
                    self.op('V', lambda E: E.tensor_scalar(ca[:], modp[:], 0.25, None, ALU.max))
                    self.op('V', lambda E: E.tensor_scalar(cb[:], modp[:], -0.25, None, ALU.min))
                    self.op('V', lambda E: E.tensor_scalar(cm[:], modp[:], 0.0, None, ALU.is_ge))
                    self.drain('V')
                    self.op('V', lambda E: E.tensor_tensor(cd[:], ca[:], cb[:], ALU.subtract))
                    self.drain('V')
                    self.op('V', lambda E: E.scalar_tensor_tensor(
                        ca[:], cm[:], 1.0, cd[:], ALU.mult, ALU.mult))
                    self.drain('V')
                    self.op('V', lambda E: E.tensor_tensor(modp[:], ca[:], cb[:], ALU.add))
                    self.drain('V')
                self.op('V', lambda E: E.engine_nop(), chV)
                self.wait('Y', chV)
                self.dma('Y', self.mod_scr[:, nb * 512:(nb + 1) * 512], modp[:], dO)
                self.wait('V', dO)

    def _wprep_units(self, weights, sc):
        chT, chV = self.chT, self.chV
        wld = [self.sbuf("swl0", [128, 1024], BF16, sc),
               self.sbuf("swl1", [128, 1024], BF16, sc)]
        slab = [self.sbuf("ssl0", [128, 8, 128], BF16, sc),
                self.sbuf("ssl1", [128, 8, 128], BF16, sc)]
        dWq = [self.chan("sw0"), self.chan("sw1")]
        dSl = [self.chan("ss0"), self.chan("ss1")]
        self._wp_store_chans.extend(dSl)
        chunks = []
        for w_d, wT_scr, n_rc, W in weights:
            wT_r = wT_scr.rearrange("(c p) i -> p c i", p=128)
            for rc in range(n_rc):
                for co in range(0, W, 1024):
                    nc = min(1024, W - co)
                    chunks.append(
                        (w_d[rc * 128:(rc + 1) * 128, co:co + nc],
                         wT_r[:, co // 128:(co + nc) // 128,
                              rc * 128:(rc + 1) * 128],
                         nc))
        qd_marks = {}
        stored = {}

        def load(gc):
            if gc >= 2:
                self.wait('G', chV, qd_marks[gc - 2])
            src, _, nc = chunks[gc]
            self.dma('G', wld[gc % 2][:, 0:nc], src, dWq[gc % 2])

        load(0)
        for gc, (src, dst, nc) in enumerate(chunks):
            if gc + 1 < len(chunks):
                load(gc + 1)
            self.wait('V', dWq[gc % 2])
            buf = wld[gc % 2]
            for po in range(0, nc, 512):
                pw = min(512, nc - po)
                self._emit_qd([(buf[:, po:po + pw], buf[:, po:po + pw], pw)])
            self.op('V', lambda E: E.engine_nop(), chV)
            qmark = chV.n
            yield
            nt = nc // 128
            self.wait('T', chV, qmark)
            self.waitm('T', self.r_pH.guard())
            for hc in range(nt):
                self.op('T', lambda E, b_=buf, hh=hc: E.transpose(
                    self.pH[:, hh, :], b_[:, hh * 128:(hh + 1) * 128],
                    self.identb[:]), chT if hc == nt - 1 else None)
            self.wait('V', chT)
            if stored.get(gc % 2):
                self.wait('V', dSl[gc % 2])
            self.op('V', lambda E, n=nt, sl=slab[gc % 2]: E.tensor_copy(
                sl[:, 0:n, :], self.pH[:, 0:n, :]), chV)
            self.drain('V')
            self.r_pH.done(chV)
            qd_marks[gc] = chV.n
            self.wait('Y', chV)
            self.dma('Y', dst, slab[gc % 2][:, 0:nt, :], dSl[gc % 2])
            stored[gc % 2] = True
            yield

    def _emit_wprep(self, w_d, wT_scr, n_chunks, W, tag):
        chT, chV, dO = self.chT, self.chV, self.dO
        n_out_c = W // 128
        with ExitStack() as sc:
            wld = [self.sbuf(f"wld0{tag}", [128, W], F32, sc),
                   self.sbuf(f"wld1{tag}", [128, W], F32, sc)]
            wqd = self.sbuf(f"wqd{tag}", [128, W], BF16, sc)
            stg = self.sbuf(f"stg{tag}", [128, n_out_c, n_chunks * 128], BF16, sc)
            dWp = [self.chan(f"dW0{tag}"), self.chan(f"dW1{tag}")]
            qd_marks = {}
            for wc in range(n_chunks):
                buf = wld[wc % 2]
                if wc >= 2:
                    self.wait('Y', chV, qd_marks[wc - 2])
                self.dma('Y', buf[:], w_d[wc * 128:(wc + 1) * 128, :], dWp[wc % 2])
                self.wait('V', dWp[wc % 2])
                pos = list(range(0, W, 512))
                for i in range(0, len(pos), 2):
                    pcs = [(buf[:, po:po + min(512, W - po)],
                            wqd[:, po:po + min(512, W - po)],
                            min(512, W - po)) for po in pos[i:i + 2]]
                    self._emit_qd(pcs)
                self.op('V', lambda E: E.engine_nop(), chV)
                qd_marks[wc] = chV.n
                for g0 in range(0, n_out_c, 8):
                    g1 = min(g0 + 8, n_out_c)
                    self.wait('T', chV)
                    self.waitm('T', self.r_pH.guard())
                    for hc in range(g0, g1):
                        self.op('T', lambda E, hh=hc, g=g0: E.transpose(
                            self.pH[:, hh - g, :], wqd[:, hh * 128:(hh + 1) * 128],
                            self.identb[:]), chT if hc == g1 - 1 else None)
                    self.wait('V', chT)
                    for hc in range(g0, g1):
                        self.op('V', lambda E, hh=hc, g=g0, w=wc: E.tensor_copy(
                            stg[:, hh, w * 128:(w + 1) * 128], self.pH[:, hh - g, :]),
                            chV if hc == g1 - 1 else None)
                    self.drain('V')
                    self.r_pH.done(chV)
            self.wait('Y', chV)
            self.dma('Y', wT_scr.rearrange("(c p) i -> p c i", p=128), stg[:], dO)
            self.wait('Y', dO)

    def _emit_qd(self, pieces):
        V, dr = self.op, self.drain
        ctx = []
        for (srcp, dstp, pw), s in zip(pieces, self.qds):
            nbl = pw // 16
            src3 = srcp.rearrange("p (a b) -> p a b", b=16)
            rs_b = s['rs'][:, 0:nbl].unsqueeze(2).broadcast_to([128, nbl, 16])
            sc_b = s['scb'][:, 0:nbl].unsqueeze(2).broadcast_to([128, nbl, 16])
            ctx.append((srcp, dstp, pw, nbl, src3, rs_b, sc_b, s))
        def each(fn):
            for c in ctx:
                fn(*c)
        dr('V')
        each(lambda srcp, dstp, pw, nbl, src3, rs_b, sc_b, s:
             V('V', lambda E: E.tensor_reduce(s['bm'][:, 0:nbl], src3, AX.X,
                                              ALU.max, apply_absolute_value=True)))
        dr('V')
        each(lambda srcp, dstp, pw, nbl, src3, rs_b, sc_b, s:
             V('V', lambda E: E.reciprocal(s['rs'][:, 0:nbl], s['bm'][:, 0:nbl])))
        each(lambda srcp, dstp, pw, nbl, src3, rs_b, sc_b, s:
             V('V', lambda E: E.tensor_scalar(s['scb'][:, 0:nbl], s['bm'][:, 0:nbl],
                                              1.0 / 6.0, 1e-12, ALU.mult, ALU.max)))
        dr('V')
        each(lambda srcp, dstp, pw, nbl, src3, rs_b, sc_b, s:
             V('V', lambda E: E.scalar_tensor_tensor(
                 s['xn'][:, 0:pw].rearrange("p (a b) -> p a b", b=16), src3, 12.0,
                 rs_b, ALU.mult, ALU.mult)))
        dr('V')
        V('V', lambda E: E.engine_nop(), self.chV)
        self.qd_srcfree = self.chV.n
        each(lambda srcp, dstp, pw, nbl, src3, rs_b, sc_b, s:
             V('V', lambda E: E.tensor_tensor(s['A'][:, 0:pw], s['xn'][:, 0:pw],
                                              s['xn'][:, 0:pw], ALU.mult)))
        each(lambda srcp, dstp, pw, nbl, src3, rs_b, sc_b, s:
             V('V', lambda E: E.tensor_scalar(s['xn'][:, 0:pw], s['xn'][:, 0:pw],
                                              0.0, 0.5, ALU.is_ge, ALU.subtract)))
        dr('V')
        for reg, thr, wgt in (('q', 0.25, None), ('q2', 2.25, None),
                              ('g', 6.25, None), ('g2', 12.25, None),
                              ('ss', 25.0, 2.0), ('t5', 49.0, 2.0),
                              ('A', 100.0, 4.0)):
            if wgt is None:
                each(lambda srcp, dstp, pw, nbl, src3, rs_b, sc_b, s,
                     r=reg, t=thr:
                     V('V', lambda E: E.tensor_scalar(
                         s[r][:, 0:pw], s['A'][:, 0:pw], t, None, ALU.is_gt)))
            else:
                each(lambda srcp, dstp, pw, nbl, src3, rs_b, sc_b, s,
                     r=reg, t=thr, w=wgt:
                     V('V', lambda E: E.tensor_scalar(
                         s[r][:, 0:pw], s['A'][:, 0:pw], t, w, ALU.is_gt,
                         ALU.mult)))
        dr('V')
        for d, a, b2 in (('q', 'q', 'q2'), ('g', 'g', 'g2'), ('ss', 'ss', 't5')):
            each(lambda srcp, dstp, pw, nbl, src3, rs_b, sc_b, s,
                 dd=d, aa=a, bb=b2:
                 V('V', lambda E: E.tensor_tensor(
                     s[dd][:, 0:pw], s[aa][:, 0:pw], s[bb][:, 0:pw], ALU.add)))
        dr('V')
        for d, a, b2 in (('q', 'q', 'g'), ('ss', 'ss', 'A')):
            each(lambda srcp, dstp, pw, nbl, src3, rs_b, sc_b, s,
                 dd=d, aa=a, bb=b2:
                 V('V', lambda E: E.tensor_tensor(
                     s[dd][:, 0:pw], s[aa][:, 0:pw], s[bb][:, 0:pw], ALU.add)))
        each(lambda srcp, dstp, pw, nbl, src3, rs_b, sc_b, s:
             V('V', lambda E: E.tensor_tensor(
                 s['t5'][:, 0:pw].rearrange("p (a b) -> p a b", b=16),
                 s['xn'][:, 0:pw].rearrange("p (a b) -> p a b", b=16), sc_b,
                 ALU.mult)))
        dr('V')
        each(lambda srcp, dstp, pw, nbl, src3, rs_b, sc_b, s:
             V('V', lambda E: E.tensor_tensor(s['q'][:, 0:pw], s['q'][:, 0:pw],
                                              s['ss'][:, 0:pw], ALU.add)))
        dr('V')
        each(lambda srcp, dstp, pw, nbl, src3, rs_b, sc_b, s:
             V('V', lambda E: E.tensor_tensor(dstp, s['q'][:, 0:pw],
                                              s['t5'][:, 0:pw], ALU.mult)))
        dr('V')

    def _emit_ln_stats(self, x_ap):
        chV, chS = self.chV, self.chS
        self.drain('V')
        self.op('V', lambda E: E.bn_stats(self.bns[:, 0:6], x_ap[:, 0:512]))
        self.op('V', lambda E: E.bn_stats(self.bns[:, 6:12], x_ap[:, 512:1024]))
        self.drain('V')
        self.op('V', lambda E: E.bn_aggr(self.mv2[:], self.bns[:]), chV)
        self.wait('S', chV)
        self.drain('S')
        self.op('S', lambda E: E.activation(self.lnv[:], self.mv2[:, 1:2], AF.Ln, bias=1e-5))
        self.drain('S')
        self.op('S', lambda E: E.activation(self.rstd[:], self.lnv[:], AF.Exp, scale=-0.5), chS)
        return chS.n

    def _emit_ln_apply(self, x_ap, out_ap, smark):
        chV, chS = self.chV, self.chS
        self.drain('V')
        self.op('V', lambda E: E.scalar_tensor_tensor(
            self.tmp_t[:], x_ap, self.mv2[:, 0:1], self.CA1[:], ALU.subtract, ALU.mult))
        self.drain('V')
        self.wait('V', chS, smark)
        self.op('V', lambda E: E.scalar_tensor_tensor(
            out_ap, self.tmp_t[:], self.rstd[:], self.SAt[:], ALU.mult, ALU.add), chV)
        self.drain('V')

    def _emit_ln_mod(self, x_ap, out_ap):
        smark = self._emit_ln_stats(x_ap)
        self._emit_ln_apply(x_ap, out_ap, smark)

    def _emit_bcast(self, rhs2, dst_sb, plus1=False, mk=None):
        chT, chV, chS = self.chT, self.chV, self.chS
        self.waitm('T', mk)
        self.wait('T', chV)
        self.wait('T', chS)
        for half in range(2):
            self.op('T', lambda E, hh=half: E.matmul(
                self.pG[:, hh * 512:(hh + 1) * 512], self.ones2f[:],
                rhs2[:, hh * 512:(hh + 1) * 512], start=True, stop=True),
                chT if half == 1 else None)
        self.wait('S', chT)
        self.drain('S')
        self.op('S', lambda E, p1=(1.0 if plus1 else 0.0): E.activation(
            dst_sb[:], self.pG[:, 0:1024], AF.Copy, bias=p1), chS)

    def _alloc_attn(self, sc):
        T, NTC = self.T, self.NTC
        b = lambda n, s, d=BF16: self.sbuf(n, s, d, sc)
        self.cosb = b("cosb", [128, T])
        self.sinb = b("sinb", [128, T])
        self.pswb = b("pswb", [128, 128])
        usz = max(8 * T, 14464)
        self.union = b("union", [128, usz])
        u = self.union
        self.h_fm = u[:, 0:8 * T].rearrange("p (c t) -> p c t", c=8)
        self.o_sb = u[:, 0:8 * 512].rearrange("p (h t) -> p h t", h=8)
        self.e_bf = [u[:, 4096:4608], u[:, 4608:5120]]
        self.diagb = u[:, 5120:5632]
        self.recipb = self.sbuf("recipb", [128, 512], F32, sc)
        self.wo_w = u[:, 6272:14464].rearrange("p (c d) -> p c d", c=8)
        self.q_fm = b("q_fm", [128, NH, T])
        self.k_fm = b("k_fm", [128, NH, T])
        self.v_tm = b("v_tm", [128, NTC, D])
        self.stag = b("stag", [128, NH, 512])
        if NTC * D >= 8192:
            vflat = self.v_tm[:, :, :].rearrange("p a b -> p (a b)")
            self.stag2 = vflat[:, 0:4096].rearrange("p (h t) -> p h t", h=NH)
            self.ropet = vflat[:, 4096:8192].rearrange("p (h t) -> p h t", h=NH)
        else:
            self.stag2 = b("stag2", [128, NH, 512])
            self.ropet = b("ropet", [128, NH, 512])
        self.wbuf = b("wbuf", [128, 8, 4, DH])
        self.hmod_t = self.stag[:, 0:2, :].rearrange("p a b -> p (a b)")
        self._a3_tail = {}

    def _emit_attention(self, s):
        T, NTC, NQB = self.T, self.NTC, self.NQB
        chT, chV, chS = self.chT, self.chV, self.chS
        dY, dG, dO, dX, dR = self.dY, self.dG, self.dO, self.dX, self.dR
        pA, pB, pC, pD, pEF, pG, pH = (self.pA, self.pB, self.pC, self.pD,
                                       self.pEF, self.pG, self.pH)
        mrow = lambda a, b: self.mod_scr[s, a:b].rearrange("(o d) -> o d", o=1)

        self.wait('Y', dO)
        self.wait('Y', chT)
        mk = self.dma_pool('Y', self.rhs2z[0:1, :], mrow(1024, 2048))
        self._emit_bcast(self.rhs2z, self.CA1, plus1=True, mk=mk)
        self.wait('Y', chT)
        mk = self.dma_pool('Y', self.rhs2z[0:1, :], mrow(0, 1024))
        self._emit_bcast(self.rhs2z, self.SAt, mk=mk)
        self.wait('Y', chT)
        mk = self.dma_pool('Y', self.rhs2z[0:1, :], mrow(2048, 3072))
        self._emit_bcast(self.rhs2z, self.GAt, mk=mk)

        def a1_stats(tc):
            b = tc % 2
            self.waitm('Y', self.r_xt.guard())
            self.dma('Y', self.xt[b][:], self.x_in[s, tc * 128:(tc + 1) * 128, :], dX[b])
            self.wait('V', dX[b])
            return self._emit_ln_stats(self.xt[b][:])

        smark = a1_stats(0)
        for tc in range(NTC):
            b = tc % 2
            self._emit_ln_apply(self.xt[b][:], self.hmod_t[:], smark)
            self.r_xt.done(chV)
            if tc + 1 < NTC:
                smark = a1_stats(tc + 1)
            self.wait('T', chV)
            self.waitm('T', self.r_pH.guard())
            for c in range(8):
                self.op('T', lambda E, cc=c: E.transpose(
                    pH[:, cc, :], self.hmod_t[:, cc * 128:(cc + 1) * 128],
                    self.identb[:]), chT if c == 7 else None)
            self.wait('V', chT)
            self.op('V', lambda E, t=tc: E.tensor_copy(
                self.h_fm[:, :, t * 128:(t + 1) * 128], pH[:]), chV)
            self.drain('V')
            self.r_pH.done(chV)

        NTB = T // 512
        qk_ring = Ring(2)
        swap_ring = Ring(2)
        for w_d, dst, scale in [(self.wq_d, self.q_fm, 1.0 / 128.0),
                                (self.wk_d, self.k_fm, 1.0)]:
            for hf in range(2):
                self.wait('G', chT)
                for h4 in range(4):
                    self.dma('G', self.wbuf[:, :, h4, :],
                             w_d[hf * 4 + h4].rearrange("(c p) d -> p c d", p=128), dG)
                self.wait('T', dG)
                self.wait('T', chV)
                for tb in range(NTB):
                    for h4 in range(4):
                        h = hf * 4 + h4
                        ps = pA if h4 % 2 == 0 else pB
                        self.waitm('T', qk_ring.guard())
                        for c in range(8):
                            self.op('T', lambda E, cc=c, hh=h4, t=tb, p=ps: E.matmul(
                                p[:], self.wbuf[:, cc, hh, :],
                                self.h_fm[:, cc, t * 512:(t + 1) * 512],
                                start=(cc == 0), stop=(cc == 7)),
                                chT if c == 7 else None)
                        self.wait('V', chT)
                        self.op('V', lambda E, hh=h, p=ps, sc_=scale: E.tensor_scalar(
                            self.stag[:, hh, :], p[:], sc_, None, ALU.mult), chV)
                        qk_ring.done(chV)
                    self._emit_rope(dst, tb, swap_ring, hf)
        v_ring = Ring(1)
        for hf in range(2):
            self.wait('G', chT)
            for h4 in range(4):
                self.dma('G', self.wbuf[:, :, h4, :],
                         self.wv_d[hf * 4 + h4].rearrange("(c p) d -> p c d", p=128), dG)
            self.wait('T', dG)
            wflat = self.wbuf[:, :, :, :].rearrange("p c h d -> p c (h d)")
            for tc in range(NTC):
                self.waitm('T', v_ring.guard())
                for c in range(8):
                    self.op('T', lambda E, cc=c, t=tc, wf=wflat: E.matmul(
                        pG[:, 0:512], self.h_fm[:, cc, t * 128:(t + 1) * 128],
                        wf[:, cc, 0:512],
                        start=(cc == 0), stop=(cc == 7)),
                        chT if c == 7 else None)
                self.wait('V', chT)
                self.op('V', lambda E, t=tc, o=hf * 512: E.tensor_copy(
                    self.v_tm[:, t, o:o + 512], pG[:, 0:512]), chV)
                self.drain('V')
                v_ring.done(chV)

        self.wait('G', chT)
        self.dma('G', self.wo_w[:, :, :],
                 self.wo_d.rearrange("(c p) d -> p c d", p=128), dG)
        self.dma('G', self.diagb[:], self.diag_d, dG)
        self.wait('T', dG)

        sc_ring = Ring(2)
        og_ring = Ring(1)
        self.wait('T', chV)
        tailst = self._a3_tail

        def sums_row(par, co=0):
            return pEF[0:1, co:512] if par == 0 else pG[0:1, 512 + co:1024]

        def emit_tail():
            t = tailst.pop('t', None)
            if t is None:
                return
            par, pO_, stop_mark, hh = t
            self.wait('V', chT, stop_mark)
            self.drain('V')
            if tailst.get('bcast') is not None:
                self.wait('V', chT, tailst['bcast'])
            self.op('V', lambda E, r=sums_row(par): E.reciprocal(
                self.rhs2z[0:1, 0:512], r), chV)
            rec_mark = chV.n
            self.wait('T', chV, rec_mark)
            if tailst.get('copy') is not None:
                self.wait('T', chV, tailst['copy'])
            if tailst.get('og') is not None:
                self.wait('T', chV, tailst['og'])
            self.op('T', lambda E: E.matmul(pG[:, 0:512], self.ones2f[:],
                                            self.rhs2z[:, 0:512],
                                            start=True, stop=True), chT)
            tailst['bcast'] = chT.n
            self.wait('V', chT)
            self.drain('V')
            self.op('V', lambda E: E.tensor_copy(self.recipb[:], pG[:, 0:512]), chV)
            tailst['copy'] = chV.n
            self.drain('V')
            self.op('V', lambda E, hh=hh, p=pO_: E.tensor_tensor(
                self.o_sb[:, hh, :], p[:], self.recipb[:], ALU.mult), chV)
            self.drain('V')
            tailst['ev%d' % par] = chV.n
            tailst['ln%d' % par] = rec_mark

        for qb in range(NQB):
            for h in range(NH):
                par = h % 2
                pO = pC if h % 2 == 0 else pD
                kcs = [(kc, 0, False) for kc in range(4 * qb)]
                kcs += [(4 * qb + d, 128 * d, True) for d in range(4)]
                if tailst.get('ln%d' % par) is not None:
                    self.wait('T', chV, tailst['ln%d' % par])
                if par == 1 and tailst.get('og') is not None:
                    self.wait('T', chV, tailst['og'])
                if tailst.get('ev%d' % par) is not None:
                    self.wait('T', chV, tailst['ev%d' % par])
                sc_marks = {}
                tt_mark = {}

                def emit_scores(j):
                    kc, coff, isdiag = kcs[j]
                    pS = pA if j % 2 == 0 else pB
                    self.waitm('T', sc_ring.guard())
                    self.op('T', lambda E, k_=kc, q_=qb, co=coff, n_=512 - coff,
                            p=pS, hh=h, sp=(not isdiag):
                            E.matmul(p[:, 0:n_],
                                     self.k_fm[:, hh, k_ * 128:(k_ + 1) * 128],
                                     self.q_fm[:, hh, q_ * 512 + co:(q_ + 1) * 512],
                                     start=True, stop=sp),
                            None if isdiag else chT)
                    if isdiag:
                        self.op('T', lambda E, n_=512 - coff, p=pS: E.matmul(
                            p[:, 0:n_], self.identb[:], self.diagb[:, 0:n_],
                            start=False, stop=True), chT)
                    sc_marks[j] = chT.n

                emit_scores(0)
                for j, (kc, coff, isdiag) in enumerate(kcs):
                    ncols = 512 - coff
                    last = (j == len(kcs) - 1)
                    pS = pA if j % 2 == 0 else pB
                    eb = self.e_bf[j % 2]
                    if j + 1 < len(kcs):
                        emit_scores(j + 1)
                    self.wait('S', chT, sc_marks[j])
                    self.op('S', lambda E, p=pS, n_=ncols, b=eb:
                            E.activation(b[:, 0:n_], p[:, 0:n_], AF.Exp), chS)
                    self.wait('T', chS)
                    self.op('T', lambda E, k_=kc, co=coff, n_=ncols, b=eb, hh=h,
                            st=(j == 0), sp=last, p=pO:
                            E.matmul(p[:, co:512],
                                     self.v_tm[:, k_, hh * DH:(hh + 1) * DH],
                                     b[:, 0:n_], start=st, stop=sp), chT)
                    self.op('T', lambda E, co=coff, n_=ncols, b=eb,
                            st=(j == 0), sp=last, pr=par:
                            E.matmul(sums_row(pr, co), self.ones_c[:], b[:, 0:n_],
                                     start=st, stop=sp), chT)
                    sc_ring.done(chT)
                new_t = (par, pO, chT.n, h)
                emit_tail()
                tailst['t'] = new_t
                next(self._wp_feed, None)
                if h == NH - 1:
                    emit_tail()
            for qc in range(4):
                tc = qb * 4 + qc
                b = tc % 2
                self.waitm('Y', self.r_xt.guard())
                self.dma('Y', self.xt[b][:], self.x_in[s, tc * 128:(tc + 1) * 128, :], dR[b])
                self.wait('T', chV)
                self.wait('T', dG)
                self.waitm('T', og_ring.guard())
                for h in range(NH):
                    for half in range(2):
                        self.op('T', lambda E, hh=h, q_=qc, ha=half: E.matmul(
                            pG[:, ha * 512:(ha + 1) * 512],
                            self.o_sb[:, hh, q_ * 128:(q_ + 1) * 128],
                            self.wo_w[:, hh, ha * 512:(ha + 1) * 512],
                            start=(hh == 0), stop=(hh == NH - 1)),
                            chT if (h == NH - 1 and half == 1) else None)
                self.wait('V', chT)
                self.wait('V', dR[b])
                self.waitm('V', self.r_xw.guard())
                self.drain('V')
                self.wait('V', chS)
                self.op('V', lambda E: E.scalar_tensor_tensor(
                    self.tmp_t[:], pG[:, 0:1024], 1.0, self.GAt[:],
                    ALU.mult, ALU.mult))
                self.drain('V')
                self.op('V', lambda E, bb=b: E.tensor_tensor(
                    self.xw[bb][:], self.tmp_t[:], self.xt[bb][:], ALU.add), chV)
                self.drain('V')
                self.r_xt.done(chV)
                og_ring.done(chV)
                self.wait('Y', chV)
                self.dma('Y', self.xmid_scr[s, tc * 128:(tc + 1) * 128, :],
                         self.xw[b][:], self.dS[b])
                self.r_xw.done_at(self.dS[b], self.dS[b].n)
            tailst['og'] = chV.n
            next(self._wp_feed, None)

    def _emit_rope(self, dst, tb, swap_ring, hf):
        chT, chV = self.chT, self.chV
        h0, h1 = hf * 4, hf * 4 + 4
        cs = self.cosb[:, tb * 512:(tb + 1) * 512].unsqueeze(1).broadcast_to([128, 4, 512])
        sn = self.sinb[:, tb * 512:(tb + 1) * 512].unsqueeze(1).broadcast_to([128, 4, 512])
        d0 = dst[:, h0:h1, tb * 512:(tb + 1) * 512]
        st = self.stag[:, h0:h1, :]
        st2 = self.stag2[:, h0:h1, :]
        rt = self.ropet[:, h0:h1, :]
        for h in range(h0, h1):
            ps = self.pC if h % 2 == 0 else self.pD
            self.waitm('T', swap_ring.guard())
            self.op('T', lambda E, hh=h, p=ps: E.matmul(
                p[:], self.pswb[:], self.stag[:, hh, :], start=True, stop=True), chT)
            self.wait('V', chT)
            self.op('V', lambda E, hh=h, p=ps: E.tensor_copy(
                self.stag2[:, hh, :], p[:]), chV)
            swap_ring.done(chV)
        self.drain('V')
        self.op('V', lambda E: E.tensor_tensor(rt, st2, sn, ALU.mult))
        self.drain('V')
        self.op('V', lambda E: E.tensor_tensor(st2, st, cs, ALU.mult))
        self.drain('V')
        self.op('V', lambda E: E.tensor_tensor(d0[:], st2, rt, ALU.add), chV)
        self.drain('V')

    def _alloc_ffn(self, sc):
        b = lambda n, s, d=BF16: self.sbuf(n, s, d, sc)
        self.wgT = b("wgT", [128, 8, I_FFN])
        self.wuT = b("wuT", [128, 8, I_FFN])
        self.wdT = b("wdT", [128, 22, D])
        self.xqf = b("xqf", [128, 8, 128])
        self.xq_t = b("xq_t", [128, D])
        self.hq_tp = [[b("hq_t00", [128, 512]), b("hq_t01", [128, 512])],
                      [b("hq_t10", [128, 512]), b("hq_t11", [128, 512])]]
        self.hqf = b("hqf", [128, 8, 128])
        self.hmid = [self.sbuf("hmid0", [128, 512], BF16, sc),
                     self.sbuf("hmid1", [128, 512], BF16, sc)]
        self.h2x = self.sbuf("h2x", [128, D], BF16, sc)
        self._qdm_prev = None
        self.dM = [self.chan("dM0"), self.chan("dM1")]

    def _emit_ffn(self, s):
        NTC = self.NTC
        chT, chV, chS = self.chT, self.chV, self.chS
        dY, dO, dM = self.dY, self.dO, self.dM
        pA, pB, pC, pD, pG, pH = self.pA, self.pB, self.pC, self.pD, self.pG, self.pH
        mrow = lambda a, b: self.mod_scr[s, a:b].rearrange("(o d) -> o d", o=1)

        self.wait('Y', chT)
        mk = self.dma_pool('Y', self.rhs2z[0:1, :], mrow(4096, 5120))
        self._emit_bcast(self.rhs2z, self.CA1, plus1=True, mk=mk)
        self.wait('Y', chT)
        mk = self.dma_pool('Y', self.rhs2z[0:1, :], mrow(3072, 4096))
        self._emit_bcast(self.rhs2z, self.SAt, mk=mk)
        self.wait('Y', chT)
        mk = self.dma_pool('Y', self.rhs2z[0:1, :], mrow(5120, 6144))
        self._emit_bcast(self.rhs2z, self.GAt, mk=mk)

        NIB = (I_FFN + 511) // 512
        gu_ring = Ring(1)
        dn_ring = Ring(1)
        for mk in self.ffn_wmk:
            self.waitm('T', mk)

        def stage_b1(tc):
            b = tc % 2
            self.waitm('Y', self.r_xt.guard())
            self.dma('Y', self.xt[b][:], self.xmid_scr[s, tc * 128:(tc + 1) * 128, :], dM[b])
            self.wait('V', dM[b])
            return self._emit_ln_stats(self.xt[b][:])

        def stage_b2(tc, smark):
            b = tc % 2
            self._emit_ln_apply(self.xt[b][:], self.h2x[:], smark)
            self._emit_qd([(self.h2x[:, 0:512], self.xq_t[:, 0:512], 512),
                           (self.h2x[:, 512:1024], self.xq_t[:, 512:1024], 512)])
            self.op('V', lambda E: E.engine_nop(), chV)
            return chV.n

        xq_ready = stage_b2(0, stage_b1(0))
        b1_mark = stage_b1(1) if NTC > 1 else None
        for tc in range(NTC):
            b = tc % 2
            self.wait('T', chV, xq_ready)
            self.waitm('T', self.r_pH.guard())
            for c in range(8):
                self.op('T', lambda E, cc=c: E.transpose(
                    pH[:, cc, :], self.xq_t[:, cc * 128:(cc + 1) * 128],
                    self.identb[:]), chT if c == 7 else None)
            self.wait('V', chT)
            self.op('V', lambda E: E.tensor_copy(self.xqf[:], pH[:]), chV)
            self.drain('V')
            self.r_pH.done(chV)
            xqf_ready = chV.n
            if tc + 1 < NTC:
                xq_ready = stage_b2(tc + 1, b1_mark)
            pairs = [[ib for ib in (ib0, ib0 + 1) if ib < NIB]
                     for ib0 in range(0, NIB, 2)]

            def emit_gu_silu(k):
                ibs = pairs[k]
                self.wait('T', chV, xqf_ready)
                self.waitm('T', gu_ring.guard())
                gms = {}
                for c in range(8):
                    for ib in ibs:
                        io = ib * 512
                        iw = min(512, I_FFN - io)
                        pGt = pA if ib % 2 == 0 else pB
                        pUt = pC if ib % 2 == 0 else pD
                        self.op('T', lambda E, cc=c, o=io, w=iw, p=pGt: E.matmul(
                            p[:, 0:w], self.xqf[:, cc, :], self.wgT[:, cc, o:o + w],
                            start=(cc == 0), stop=(cc == 7)),
                            chT if c == 7 else None)
                        if c == 7:
                            gms[ib] = chT.n
                        self.op('T', lambda E, cc=c, o=io, w=iw, p=pUt: E.matmul(
                            p[:, 0:w], self.xqf[:, cc, :], self.wuT[:, cc, o:o + w],
                            start=(cc == 0), stop=(cc == 7)),
                            chT if c == 7 else None)
                gu_mark = chT.n
                self.drain('S')
                if self._qdm_prev is not None:
                    self.wait('S', chV, self._qdm_prev)
                for i, ib in enumerate(ibs):
                    iw = min(512, I_FFN - ib * 512)
                    pGt = pA if ib % 2 == 0 else pB
                    self.wait('S', chT, gms[ib])
                    self.op('S', lambda E, w=iw, p=pGt, i=i: E.activation(
                        self.hmid[i][:, 0:w], p[:, 0:w], AF.Silu),
                        chS if ib == ibs[-1] else None)
                return gu_mark

            def emit_vblock(k, gu_mark):
                ibs = pairs[k]
                hqt = self.hq_tp[k % 2]
                self.wait('V', chS)
                self.wait('V', chT, gu_mark)
                self.drain('V')
                for i, ib in enumerate(ibs):
                    iw = min(512, I_FFN - ib * 512)
                    pUt = pC if ib % 2 == 0 else pD
                    self.op('V', lambda E, w=iw, p=pUt, i=i: E.tensor_tensor(
                        self.hmid[i][:, 0:w], self.hmid[i][:, 0:w], p[:, 0:w],
                        ALU.mult), chV if ib == ibs[-1] else None)
                gu_ring.done(chV)
                self._emit_qd([(self.hmid[i][:, 0:min(512, I_FFN - ib * 512)],
                                hqt[i][:, 0:min(512, I_FFN - ib * 512)],
                                min(512, I_FFN - ib * 512))
                               for i, ib in enumerate(ibs)])
                self.op('V', lambda E: E.engine_nop(), chV)
                return chV.n

            def emit_htdn(k, qdm):
                ibs = pairs[k]
                hqt = self.hq_tp[k % 2]
                nst_tot = sum(min(512, I_FFN - ib * 512) // 128 for ib in ibs)
                self.wait('T', chV, qdm)
                self.waitm('T', self.r_pH.guard())
                kk = 0
                for i, ib in enumerate(ibs):
                    nst = min(512, I_FFN - ib * 512) // 128
                    for c in range(nst):
                        self.op('T', lambda E, cc=c, k2=kk, hq=hqt[i]: E.transpose(
                            pH[:, k2, :], hq[:, cc * 128:(cc + 1) * 128],
                            self.identb[:]), chT if kk == nst_tot - 1 else None)
                        kk += 1
                self.wait('V', chT)
                self.op('V', lambda E, n=nst_tot: E.tensor_copy(
                    self.hqf[:, 0:n, :], pH[:, 0:n, :]), chV)
                self.drain('V')
                self.r_pH.done(chV)
                if ibs[0] == 0:
                    self.waitm('T', dn_ring.guard())
                self.wait('T', chV)
                kk = 0
                for i, ib in enumerate(ibs):
                    nst = min(512, I_FFN - ib * 512) // 128
                    for c in range(nst):
                        icg = ib * 4 + c
                        lastd = (ib == NIB - 1 and c == nst - 1)
                        for half in range(2):
                            self.op('T', lambda E, k2=kk, ig=icg, ha=half,
                                    st=(ib == 0 and c == 0), sp=lastd: E.matmul(
                                pG[:, ha * 512:(ha + 1) * 512], self.hqf[:, k2, :],
                                self.wdT[:, ig, ha * 512:(ha + 1) * 512],
                                start=st, stop=sp),
                                chT if (kk == nst_tot - 1 and half == 1) else None)
                        kk += 1

            prev = None
            for k in range(len(pairs)):
                gu_mark = emit_gu_silu(k)
                if prev is not None:
                    emit_htdn(*prev)
                qdm = emit_vblock(k, gu_mark)
                self._qdm_prev = self.qd_srcfree
                prev = (k, qdm)
            emit_htdn(*prev)
            self.wait('V', chT)
            self.waitm('V', self.r_xw.guard())
            self.drain('V')
            self.op('V', lambda E: E.tensor_tensor(
                self.tmp_t[:], pG[:, 0:1024], self.GAt[:], ALU.mult))
            self.drain('V')
            self.op('V', lambda E, bb=b: E.tensor_tensor(
                self.xw[bb][:], self.tmp_t[:], self.xt[bb][:], ALU.add), chV)
            self.drain('V')
            self.r_xt.done(chV)
            dn_ring.done(chV)
            self.wait('Y', chV)
            self.dma('Y', self.out_d[s, tc * 128:(tc + 1) * 128, :], self.xw[b][:],
                     self.dS[b])
            self.r_xw.done_at(self.dS[b], self.dS[b].n)
            if tc + 2 < NTC:
                b1_mark = stage_b1(tc + 2)


def _host_consts(T):
    i = np.arange(32, dtype=np.float64)
    t = np.arange(T, dtype=np.float64)
    freqs = np.power(10000.0, -i / 32.0)
    rad = t[None, :] * freqs[:, None]
    cos_t = np.cos(rad)
    sin_t = np.sin(rad)
    C = np.ones((128, T))
    S = np.zeros((128, T))
    C[0:32] = cos_t
    C[32:64] = cos_t
    S[0:32] = -sin_t
    S[32:64] = sin_t
    jj = np.arange(128)
    ee = np.arange(512)
    diag01 = np.where(jj[:, None] > ee[None, :], -1e30, 0.0).astype(np.float32)
    ident = np.eye(128, dtype=np.float32)
    sw = np.concatenate([np.arange(32, 64), np.arange(0, 32), np.arange(64, 128)])
    pswap = np.zeros((128, 128), np.float32)
    pswap[np.arange(128), sw] = 1.0
    return C.astype(np.float32), S.astype(np.float32), diag01, ident, pswap


def make_in_maps(L0_attn_x, L0_attn_wq, L0_attn_wk, L0_attn_wv, L0_attn_wo,
                 L0_ffn_w_gate, L0_ffn_w_up, L0_ffn_w_down, L0_c, L0_wmod,
                 T=2048, NSEQ=4, ncores=NCORES):
    ctab, stab, diag01, ident, pswap = _host_consts(T)
    asc = lambda a: np.ascontiguousarray(np.asarray(a, dtype=np.float32))
    maps = []
    for core in range(ncores):
        s0 = core * NSEQ
        maps.append({
            "x": asc(L0_attn_x[s0:s0 + NSEQ, :T]),
            "wq": asc(L0_attn_wq), "wk": asc(L0_attn_wk), "wv": asc(L0_attn_wv),
            "wo": asc(L0_attn_wo),
            "w_gate": asc(L0_ffn_w_gate), "w_up": asc(L0_ffn_w_up),
            "w_down": asc(L0_ffn_w_down),
            "wmod": asc(L0_wmod),
            "cvec": asc(L0_c[s0:s0 + NSEQ]),
            "cos_t": ctab, "sin_t": stab, "diag01": diag01, "ident": ident,
            "pswap": pswap,
        })
    return maps


def kernel(L0_attn_x, L0_attn_wq, L0_attn_wk, L0_attn_wv, L0_attn_wo,
           L0_ffn_w_gate, L0_ffn_w_up, L0_ffn_w_down, L0_c, L0_wmod):
    b = Builder(T=2048, NSEQ=4)
    in_maps = make_in_maps(L0_attn_x, L0_attn_wq, L0_attn_wk, L0_attn_wv,
                           L0_attn_wo, L0_ffn_w_gate, L0_ffn_w_up,
                           L0_ffn_w_down, L0_c, L0_wmod)
    res = run_bass_kernel_spmd(b.nc, in_maps, core_ids=list(range(NCORES)))
    return np.concatenate([r["out"] for r in res.results], axis=0)


# version 22 (final)
# speedup vs baseline: 1.0084x; 1.0084x over previous
import numpy as np
import ml_dtypes
import concourse.bass as bass
import concourse.mybir as mybir
import concourse.bass_utils as _bass_utils
from concourse.bass_utils import run_bass_kernel_spmd
from contextlib import ExitStack

if not getattr(_bass_utils, "_ldw_opt_patched", False):
    _orig_run_command = _bass_utils.run_command

    def _run_command_ldw(cmd, *a, **kw):
        if isinstance(cmd, (list, tuple)):
            cmd = ["--enable-ldw-opt=true" if c == "--enable-ldw-opt=false"
                   else c for c in cmd]
        return _orig_run_command(cmd, *a, **kw)

    _bass_utils.run_command = _run_command_ldw
    _bass_utils._ldw_opt_patched = True

F32 = mybir.dt.float32
BF16 = mybir.dt.bfloat16
AF = mybir.ActivationFunctionType
ALU = mybir.AluOpType
AX = mybir.AxisListType

D = 1024
NH = 8
DH = 128
I_FFN = 2816
NCORES = 8


class Chan:
    def __init__(self, stack, nc, name):
        self.h = stack.enter_context(nc.semaphore(name))
        self.n = 0


class Ring:
    def __init__(self, nslots):
        self.nslots = nslots
        self.marks = []

    def guard(self):
        if len(self.marks) < self.nslots:
            return None
        return self.marks[-self.nslots]

    def done(self, chan):
        self.marks.append((chan, chan.n))

    def done_at(self, chan, val):
        self.marks.append((chan, val))


class Builder:
    def __init__(self, T=2048, NSEQ=4):
        self.T = T
        self.NSEQ = NSEQ
        self.NTC = T // 128
        self.NQB = T // 512
        self.steps = []
        self.stack = ExitStack()
        self.nc = bass.Bass("TRN2", target_bir_lowering=False)
        self._build()

    def em(self, key, fn):
        self.steps.append((key, fn))

    def wait(self, key, ch, val=None):
        v = ch.n if val is None else val
        if v <= 0:
            return
        self.em(key, lambda E, s=ch.h, vv=v: E.wait_ge(s, vv))

    def waitm(self, key, mark):
        if mark is None:
            return
        ch, v = mark
        self.wait(key, ch, v)

    def op(self, key, fn, ch=None):
        if ch is None:
            self.em(key, fn)
        else:
            ch.n += 1
            self.em(key, lambda E, f=fn, s=ch.h: f(E).then_inc(s, 1))

    def drain(self, key):
        self.em(key, lambda E: E.drain())

    def dma(self, key, out_ap, in_ap, ch):
        ch.n += 16
        self.em(key, lambda E, o=out_ap, i=in_ap, s=ch.h:
                E.dma_start(out=o, in_=i).then_inc(s, 16))

    def dma_pool(self, key, out_ap, in_ap):
        ch = self.dpool[self.dpool_i % len(self.dpool)]
        self.dpool_i += 1
        if ch.n > 0:
            self.wait(key, ch)
        self.dma(key, out_ap, in_ap, ch)
        return (ch, ch.n)

    def sbuf(self, name, shape, dt=F32, stack=None):
        return (stack or self.stack).enter_context(
            self.nc.sbuf_tensor(name, shape, dt))

    def psum(self, name, shape, dt=F32):
        return self.stack.enter_context(self.nc.psum_tensor(name, shape, dt))

    def chan(self, name):
        return Chan(self.stack, self.nc, name)

    def scope_barrier(self):
        for k in ('Y', 'G'):
            self.wait(k, self.chT)
            self.wait(k, self.chV)
            self.wait(k, self.chS)
            self.wait(k, self.dO)
            self.wait(k, self.dY)
            self.wait(k, self.dG)
            self.wait(k, self.dS[0])
            self.wait(k, self.dS[1])
            for p in self.dpool:
                self.wait(k, p)

    def _build(self):
        nc, T, NSEQ, NTC, NQB = self.nc, self.T, self.NSEQ, self.NTC, self.NQB
        dram = lambda n, s, d=F32, k="ExternalInput": nc.dram_tensor(n, s, d, kind=k).ap()
        self.x_in = dram("x", [NSEQ, T, D])
        self.wq_d = dram("wq", [NH, D, DH])
        self.wk_d = dram("wk", [NH, D, DH])
        self.wv_d = dram("wv", [NH, D, DH])
        self.wo_d = dram("wo", [D, D])
        self.wg_d = dram("w_gate", [I_FFN, D])
        self.wu_d = dram("w_up", [I_FFN, D])
        self.wd_d = dram("w_down", [D, I_FFN])
        self.wm_d = dram("wmod", [D, 6 * D])
        self.c_d = dram("cvec", [NSEQ, D])
        self.cos_d = dram("cos_t", [128, T])
        self.sin_d = dram("sin_t", [128, T])
        self.psw_d = dram("pswap", [128, 128])
        self.diag_d = dram("diag01", [128, 512])
        self.id_d = dram("ident", [128, 128])
        self.out_d = dram("out", [NSEQ, T, D], F32, "ExternalOutput")
        scr = lambda n, s, d: nc.dram_tensor(n, s, d, kind="Internal").ap()
        self.mod_scr = scr("mod_scr", [NSEQ, 6 * D], F32)
        self.xmid_scr = scr("xmid_scr", [NSEQ, T, D], F32)
        self.wgT_scr = scr("wgT_scr", [D, I_FFN], BF16)
        self.wuT_scr = scr("wuT_scr", [D, I_FFN], BF16)
        self.wdT_scr = scr("wdT_scr", [I_FFN, D], BF16)

        self.chT = self.chan("chT")
        self.chV = self.chan("chV")
        self.chS = self.chan("chS")
        self.dY = self.chan("dY")
        self.dG = self.chan("dG")
        self.dO = self.chan("dO")
        self.dX = [self.chan("dX0"), self.chan("dX1")]
        self.dR = [self.chan("dR0"), self.chan("dR1")]
        self.dS = [self.chan("dS0"), self.chan("dS1")]
        self.dpool = [self.chan(f"dp{i}") for i in range(6)]
        self.dpool_i = 0


        self.identb = self.sbuf("identb", [128, 128], BF16)
        self.ones_c = self.sbuf("ones_c", [128, 1], BF16)
        self.ones2f = self.sbuf("ones2f", [2, 128], F32)
        self.rhs2z = self.sbuf("rhs2z", [2, D], F32)
        self.CA1 = self.sbuf("CA1", [128, D], F32)
        self.SAt = self.sbuf("SAt", [128, D], F32)
        self.GAt = self.sbuf("GAt", [128, D], F32)
        self.bns = self.sbuf("bns", [128, 12], F32)
        self.mv2 = self.sbuf("mv2", [128, 2], F32)
        self.rstd = self.sbuf("rstd", [128, 1], F32)
        self.lnv = self.sbuf("lnv", [128, 1], F32)
        self.cpool = self.sbuf("cpool", [128, 5], F32)
        for i, v in enumerate([1e-5, -0.5, -1.0, 2.0, 0.0078125]):
            self.nc.const_aps.register(F32, v, self.cpool[:, i:i + 1])
        self.xt = [self.sbuf("xt0", [128, D]), self.sbuf("xt1", [128, D])]
        self.xw = [self.sbuf("xw0", [128, D]), self.sbuf("xw1", [128, D])]
        self.tmp_t = self.sbuf("tmp_t", [128, D])
        self.pA = self.psum("pA", [128, 512])
        self.pB = self.psum("pB", [128, 512])
        self.pC = self.psum("pC", [128, 512])
        self.pD = self.psum("pD", [128, 512])
        self.pEF = self.psum("pEF", [128, 512])
        self.pG = self.psum("pG", [128, 1024])
        self.pH = self.psum("pH", [128, 8, 128], BF16)
        self.r_xt = Ring(2)
        self.r_xw = Ring(2)
        self.r_pH = Ring(1)

        self._init_consts()
        self.scope_barrier()
        self._emit_mod()
        self.scope_barrier()
        with ExitStack() as attn_scope:
            self._alloc_attn(attn_scope)
            self._init_tables()
            self._alloc_qd(attn_scope, "_a", nsets=1)
            self._wp_store_chans = []
            self._wp_feed = self._wprep_units(
                [(self.wg_d, self.wgT_scr, 22, 1024),
                 (self.wu_d, self.wuT_scr, 22, 1024),
                 (self.wd_d, self.wdT_scr, 8, I_FFN)],
                attn_scope)
            for s in range(NSEQ):
                self._emit_attention(s)
            for _ in self._wp_feed:
                pass
            for ch in self._wp_store_chans:
                self.wait('Y', ch)
                self.wait('G', ch)

        self.scope_barrier()
        with ExitStack() as ffn_scope:
            self._alloc_qd(ffn_scope, "_f")
            self._alloc_ffn(ffn_scope)
            self.wait('Y', self.dO)
            self.wait('Y', self.dS[0])
            self.wait('Y', self.dS[1])
            wmk = [self.dma_pool('Y', self.wgT[:], self.wgT_scr.rearrange("(c p) i -> p c i", p=128)),
                   self.dma_pool('Y', self.wuT[:], self.wuT_scr.rearrange("(c p) i -> p c i", p=128)),
                   self.dma_pool('Y', self.wdT[:], self.wdT_scr.rearrange("(c p) i -> p c i", p=128))]
            self.ffn_wmk = wmk
            for s in range(NSEQ):
                self._emit_ffn(s)

        self.wait('Y', self.dO)
        self.wait('Y', self.dS[0])
        self.wait('Y', self.dS[1])

        keymap = {'Y': 'sync', 'G': 'gpsimd', 'V': 'vector', 'S': 'scalar', 'T': 'tensor'}
        with self.stack:
            with nc.Block() as block:
                for key, attr in keymap.items():
                    fns = [f for k, f in self.steps if k == key]
                    getattr(block, attr)(lambda E, fl=tuple(fns): [f(E) for f in fl])

    def _alloc_qd(self, sc, tag, nsets=2):
        self.qds = []
        for i in range(nsets):
            s = {}
            s['xn'] = self.sbuf(f"qd_xn{i}{tag}", [128, 512], BF16, sc)
            s['A'] = self.sbuf(f"qd_A{i}{tag}", [128, 512], BF16, sc)
            s['q'] = self.sbuf(f"qd_q{i}{tag}", [128, 512], BF16, sc)
            s['q2'] = self.sbuf(f"qd_q2{i}{tag}", [128, 512], BF16, sc)
            s['g'] = self.sbuf(f"qd_g{i}{tag}", [128, 512], BF16, sc)
            s['g2'] = self.sbuf(f"qd_g2{i}{tag}", [128, 512], BF16, sc)
            s['ss'] = self.sbuf(f"qd_ss{i}{tag}", [128, 512], BF16, sc)
            s['t5'] = self.sbuf(f"qd_t5{i}{tag}", [128, 512], BF16, sc)
            s['bm'] = self.sbuf(f"qd_bm{i}{tag}", [128, 32], F32, sc)
            s['scb'] = self.sbuf(f"qd_scb{i}{tag}", [128, 32], BF16, sc)
            s['rs'] = self.sbuf(f"qd_rs{i}{tag}", [128, 32], F32, sc)
            self.qds.append(s)

    def _init_consts(self):
        idf = self.xw[1]
        mk = self.dma_pool('Y', idf[:, 0:128], self.id_d)
        self.waitm('V', mk)
        self.op('V', lambda E: E.tensor_copy(self.identb[:], idf[:, 0:128]))
        self.op('V', lambda E: E.memset(self.ones_c[:], 1.0))
        self.op('V', lambda E: E.memset(self.ones2f[:], 1.0))
        self.op('V', lambda E: E.memset(self.rhs2z[:], 0.0))
        for i, v in enumerate([1e-5, -0.5, -1.0, 2.0, 0.0078125]):
            self.op('V', lambda E, ii=i, vv=v: E.memset(self.cpool[:, ii:ii + 1], vv))
        self.drain('V')
        self.op('V', lambda E: E.engine_nop(), self.chV)
        self.r_xt.done(self.chV)
        self.r_xt.done(self.chV)
        self.r_xw.done(self.chV)
        self.r_xw.done(self.chV)

    def _init_tables(self):
        self.dma('G', self.cosb[:], self.cos_d, self.dG)
        self.dma('G', self.sinb[:], self.sin_d, self.dG)
        self.dma('G', self.pswb[:], self.psw_d, self.dG)
        self.wait('V', self.dG)
        self.wait('T', self.dG)
        self.op('V', lambda E: E.engine_nop(), self.chV)

    def _emit_mod(self):
        nc, NSEQ = self.nc, self.NSEQ
        chT, chV, chS, dY, dG, dO = self.chT, self.chV, self.chS, self.dY, self.dG, self.dO
        with ExitStack() as sc:
            cs_f = self.sbuf("cs_f", [NSEQ, 1024], F32, sc)
            cs_b = self.sbuf("cs_b", [NSEQ, 1024], BF16, sc)
            scT = self.sbuf("scT", [128, 8, NSEQ], BF16, sc)
            wmb = [self.sbuf("wmb0", [128, 8, 512], BF16, sc),
                   self.sbuf("wmb1", [128, 8, 512], BF16, sc)]
            modp = self.sbuf("modp", [NSEQ, 512], F32, sc)
            dGm = [self.chan("dGm0"), self.chan("dGm1")]
            ca = self.sbuf("cl_a", [NSEQ, 512], F32, sc)
            cb = self.sbuf("cl_b", [NSEQ, 512], F32, sc)
            cm = self.sbuf("cl_m", [NSEQ, 512], F32, sc)
            cd = self.sbuf("cl_d", [NSEQ, 512], F32, sc)

            mk = self.dma_pool('Y', cs_f[:], self.c_d)
            self.waitm('S', mk)
            self.op('S', lambda E: E.activation(cs_f[:], cs_f[:], AF.Silu), chS)
            self.wait('V', chS)
            self.op('V', lambda E: E.tensor_copy(cs_b[:], cs_f[:]), chV)
            self.wait('T', chV)
            for c in range(8):
                self.op('T', lambda E, cc=c: E.transpose(
                    self.pH[:, cc, 0:NSEQ], cs_b[0:NSEQ, cc * 128:(cc + 1) * 128],
                    self.identb[0:NSEQ, 0:NSEQ]), chT if c == 7 else None)
            self.wait('V', chT)
            self.op('V', lambda E: E.tensor_copy(scT[:], self.pH[:, :, 0:NSEQ]), chV)
            self.r_pH.done(self.chV)
            scT_ready = chV.n
            mm_marks = {}
            for nb in range(12):
                buf = wmb[nb % 2]
                if nb >= 2:
                    self.wait('G', chT, mm_marks[nb - 2])
                self.dma('G', buf[:],
                         self.wm_d[:, nb * 512:(nb + 1) * 512]
                         .rearrange("(c p) n -> p c n", p=128), dGm[nb % 2])
                self.wait('T', dGm[nb % 2])
                if nb == 0:
                    self.wait('T', chV, scT_ready)
                else:
                    self.wait('T', chV)
                for c in range(8):
                    self.op('T', lambda E, cc=c, b=buf: E.matmul(
                        self.pA[0:NSEQ, :], scT[:, cc, :], b[:, cc, :],
                        start=(cc == 0), stop=(cc == 7)), chT if c == 7 else None)
                mm_marks[nb] = chT.n
                self.wait('V', chT)
                self.op('V', lambda E: E.tensor_copy(modp[:], self.pA[0:NSEQ, :]))
                self.drain('V')
                if nb in (4, 5, 10, 11):
                    self.op('V', lambda E: E.tensor_scalar(ca[:], modp[:], 0.25, None, ALU.max))
                    self.op('V', lambda E: E.tensor_scalar(cb[:], modp[:], -0.25, None, ALU.min))
                    self.op('V', lambda E: E.tensor_scalar(cm[:], modp[:], 0.0, None, ALU.is_ge))
                    self.drain('V')
                    self.op('V', lambda E: E.tensor_tensor(cd[:], ca[:], cb[:], ALU.subtract))
                    self.drain('V')
                    self.op('V', lambda E: E.scalar_tensor_tensor(
                        ca[:], cm[:], 1.0, cd[:], ALU.mult, ALU.mult))
                    self.drain('V')
                    self.op('V', lambda E: E.tensor_tensor(modp[:], ca[:], cb[:], ALU.add))
                    self.drain('V')
                self.op('V', lambda E: E.engine_nop(), chV)
                self.wait('Y', chV)
                self.dma('Y', self.mod_scr[:, nb * 512:(nb + 1) * 512], modp[:], dO)
                self.wait('V', dO)

    def _wprep_units(self, weights, sc):
        chT, chV = self.chT, self.chV
        wld = [self.sbuf("swl0", [128, 1024], BF16, sc),
               self.sbuf("swl1", [128, 1024], BF16, sc)]
        slab = [self.sbuf("ssl0", [128, 8, 128], BF16, sc),
                self.sbuf("ssl1", [128, 8, 128], BF16, sc)]
        dWq = [self.chan("sw0"), self.chan("sw1")]
        dSl = [self.chan("ss0"), self.chan("ss1")]
        self._wp_store_chans.extend(dSl)
        chunks = []
        for w_d, wT_scr, n_rc, W in weights:
            wT_r = wT_scr.rearrange("(c p) i -> p c i", p=128)
            for rc in range(n_rc):
                for co in range(0, W, 1024):
                    nc = min(1024, W - co)
                    chunks.append(
                        (w_d[rc * 128:(rc + 1) * 128, co:co + nc],
                         wT_r[:, co // 128:(co + nc) // 128,
                              rc * 128:(rc + 1) * 128],
                         nc))
        qd_marks = {}
        stored = {}

        def load(gc):
            if gc >= 2:
                self.wait('G', chV, qd_marks[gc - 2])
            src, _, nc = chunks[gc]
            self.dma('G', wld[gc % 2][:, 0:nc], src, dWq[gc % 2])

        load(0)
        for gc, (src, dst, nc) in enumerate(chunks):
            if gc + 1 < len(chunks):
                load(gc + 1)
            self.wait('V', dWq[gc % 2])
            buf = wld[gc % 2]
            for po in range(0, nc, 512):
                pw = min(512, nc - po)
                self._emit_qd([(buf[:, po:po + pw], buf[:, po:po + pw], pw)])
            self.op('V', lambda E: E.engine_nop(), chV)
            qmark = chV.n
            yield
            nt = nc // 128
            self.wait('T', chV, qmark)
            self.waitm('T', self.r_pH.guard())
            for hc in range(nt):
                self.op('T', lambda E, b_=buf, hh=hc: E.transpose(
                    self.pH[:, hh, :], b_[:, hh * 128:(hh + 1) * 128],
                    self.identb[:]), chT if hc == nt - 1 else None)
            self.wait('V', chT)
            if stored.get(gc % 2):
                self.wait('V', dSl[gc % 2])
            self.op('V', lambda E, n=nt, sl=slab[gc % 2]: E.tensor_copy(
                sl[:, 0:n, :], self.pH[:, 0:n, :]), chV)
            self.drain('V')
            self.r_pH.done(chV)
            qd_marks[gc] = chV.n
            self.wait('Y', chV)
            self.dma('Y', dst, slab[gc % 2][:, 0:nt, :], dSl[gc % 2])
            stored[gc % 2] = True
            yield

    def _emit_wprep(self, w_d, wT_scr, n_chunks, W, tag):
        chT, chV, dO = self.chT, self.chV, self.dO
        n_out_c = W // 128
        with ExitStack() as sc:
            wld = [self.sbuf(f"wld0{tag}", [128, W], F32, sc),
                   self.sbuf(f"wld1{tag}", [128, W], F32, sc)]
            wqd = self.sbuf(f"wqd{tag}", [128, W], BF16, sc)
            stg = self.sbuf(f"stg{tag}", [128, n_out_c, n_chunks * 128], BF16, sc)
            dWp = [self.chan(f"dW0{tag}"), self.chan(f"dW1{tag}")]
            qd_marks = {}
            for wc in range(n_chunks):
                buf = wld[wc % 2]
                if wc >= 2:
                    self.wait('Y', chV, qd_marks[wc - 2])
                self.dma('Y', buf[:], w_d[wc * 128:(wc + 1) * 128, :], dWp[wc % 2])
                self.wait('V', dWp[wc % 2])
                pos = list(range(0, W, 512))
                for i in range(0, len(pos), 2):
                    pcs = [(buf[:, po:po + min(512, W - po)],
                            wqd[:, po:po + min(512, W - po)],
                            min(512, W - po)) for po in pos[i:i + 2]]
                    self._emit_qd(pcs)
                self.op('V', lambda E: E.engine_nop(), chV)
                qd_marks[wc] = chV.n
                for g0 in range(0, n_out_c, 8):
                    g1 = min(g0 + 8, n_out_c)
                    self.wait('T', chV)
                    self.waitm('T', self.r_pH.guard())
                    for hc in range(g0, g1):
                        self.op('T', lambda E, hh=hc, g=g0: E.transpose(
                            self.pH[:, hh - g, :], wqd[:, hh * 128:(hh + 1) * 128],
                            self.identb[:]), chT if hc == g1 - 1 else None)
                    self.wait('V', chT)
                    for hc in range(g0, g1):
                        self.op('V', lambda E, hh=hc, g=g0, w=wc: E.tensor_copy(
                            stg[:, hh, w * 128:(w + 1) * 128], self.pH[:, hh - g, :]),
                            chV if hc == g1 - 1 else None)
                    self.drain('V')
                    self.r_pH.done(chV)
            self.wait('Y', chV)
            self.dma('Y', wT_scr.rearrange("(c p) i -> p c i", p=128), stg[:], dO)
            self.wait('Y', dO)

    def _emit_qd(self, pieces):
        V, dr = self.op, self.drain
        ctx = []
        for (srcp, dstp, pw), s in zip(pieces, self.qds):
            nbl = pw // 16
            src3 = srcp.rearrange("p (a b) -> p a b", b=16)
            rs_b = s['rs'][:, 0:nbl].unsqueeze(2).broadcast_to([128, nbl, 16])
            sc_b = s['scb'][:, 0:nbl].unsqueeze(2).broadcast_to([128, nbl, 16])
            ctx.append((srcp, dstp, pw, nbl, src3, rs_b, sc_b, s))
        def each(fn):
            for c in ctx:
                fn(*c)
        dr('V')
        each(lambda srcp, dstp, pw, nbl, src3, rs_b, sc_b, s:
             V('V', lambda E: E.tensor_reduce(s['bm'][:, 0:nbl], src3, AX.X,
                                              ALU.max, apply_absolute_value=True)))
        dr('V')
        each(lambda srcp, dstp, pw, nbl, src3, rs_b, sc_b, s:
             V('V', lambda E: E.reciprocal(s['rs'][:, 0:nbl], s['bm'][:, 0:nbl])))
        each(lambda srcp, dstp, pw, nbl, src3, rs_b, sc_b, s:
             V('V', lambda E: E.tensor_scalar(s['scb'][:, 0:nbl], s['bm'][:, 0:nbl],
                                              1.0 / 6.0, 1e-12, ALU.mult, ALU.max)))
        dr('V')
        each(lambda srcp, dstp, pw, nbl, src3, rs_b, sc_b, s:
             V('V', lambda E: E.scalar_tensor_tensor(
                 s['xn'][:, 0:pw].rearrange("p (a b) -> p a b", b=16), src3, 12.0,
                 rs_b, ALU.mult, ALU.mult)))
        dr('V')
        V('V', lambda E: E.engine_nop(), self.chV)
        self.qd_srcfree = self.chV.n
        each(lambda srcp, dstp, pw, nbl, src3, rs_b, sc_b, s:
             V('V', lambda E: E.tensor_tensor(s['A'][:, 0:pw], s['xn'][:, 0:pw],
                                              s['xn'][:, 0:pw], ALU.mult)))
        each(lambda srcp, dstp, pw, nbl, src3, rs_b, sc_b, s:
             V('V', lambda E: E.tensor_scalar(s['xn'][:, 0:pw], s['xn'][:, 0:pw],
                                              0.0, 0.5, ALU.is_ge, ALU.subtract)))
        dr('V')
        for reg, thr, wgt in (('q', 0.25, None), ('q2', 2.25, None),
                              ('g', 6.25, None), ('g2', 12.25, None),
                              ('ss', 25.0, 2.0), ('t5', 49.0, 2.0),
                              ('A', 100.0, 4.0)):
            if wgt is None:
                each(lambda srcp, dstp, pw, nbl, src3, rs_b, sc_b, s,
                     r=reg, t=thr:
                     V('V', lambda E: E.tensor_scalar(
                         s[r][:, 0:pw], s['A'][:, 0:pw], t, None, ALU.is_gt)))
            else:
                each(lambda srcp, dstp, pw, nbl, src3, rs_b, sc_b, s,
                     r=reg, t=thr, w=wgt:
                     V('V', lambda E: E.tensor_scalar(
                         s[r][:, 0:pw], s['A'][:, 0:pw], t, w, ALU.is_gt,
                         ALU.mult)))
        dr('V')
        for d, a, b2 in (('q', 'q', 'q2'), ('g', 'g', 'g2'), ('ss', 'ss', 't5')):
            each(lambda srcp, dstp, pw, nbl, src3, rs_b, sc_b, s,
                 dd=d, aa=a, bb=b2:
                 V('V', lambda E: E.tensor_tensor(
                     s[dd][:, 0:pw], s[aa][:, 0:pw], s[bb][:, 0:pw], ALU.add)))
        dr('V')
        for d, a, b2 in (('q', 'q', 'g'), ('ss', 'ss', 'A')):
            each(lambda srcp, dstp, pw, nbl, src3, rs_b, sc_b, s,
                 dd=d, aa=a, bb=b2:
                 V('V', lambda E: E.tensor_tensor(
                     s[dd][:, 0:pw], s[aa][:, 0:pw], s[bb][:, 0:pw], ALU.add)))
        each(lambda srcp, dstp, pw, nbl, src3, rs_b, sc_b, s:
             V('V', lambda E: E.tensor_tensor(
                 s['t5'][:, 0:pw].rearrange("p (a b) -> p a b", b=16),
                 s['xn'][:, 0:pw].rearrange("p (a b) -> p a b", b=16), sc_b,
                 ALU.mult)))
        dr('V')
        each(lambda srcp, dstp, pw, nbl, src3, rs_b, sc_b, s:
             V('V', lambda E: E.tensor_tensor(s['q'][:, 0:pw], s['q'][:, 0:pw],
                                              s['ss'][:, 0:pw], ALU.add)))
        dr('V')
        each(lambda srcp, dstp, pw, nbl, src3, rs_b, sc_b, s:
             V('V', lambda E: E.tensor_tensor(dstp, s['q'][:, 0:pw],
                                              s['t5'][:, 0:pw], ALU.mult)))
        dr('V')

    def _emit_ln_stats(self, x_ap):
        chV, chS = self.chV, self.chS
        self.drain('V')
        self.op('V', lambda E: E.bn_stats(self.bns[:, 0:6], x_ap[:, 0:512]))
        self.op('V', lambda E: E.bn_stats(self.bns[:, 6:12], x_ap[:, 512:1024]))
        self.drain('V')
        self.op('V', lambda E: E.bn_aggr(self.mv2[:], self.bns[:]), chV)
        self.wait('S', chV)
        self.drain('S')
        self.op('S', lambda E: E.activation(self.lnv[:], self.mv2[:, 1:2], AF.Ln, bias=1e-5))
        self.drain('S')
        self.op('S', lambda E: E.activation(self.rstd[:], self.lnv[:], AF.Exp, scale=-0.5), chS)
        return chS.n

    def _emit_ln_apply(self, x_ap, out_ap, smark):
        chV, chS = self.chV, self.chS
        self.drain('V')
        self.op('V', lambda E: E.scalar_tensor_tensor(
            self.tmp_t[:], x_ap, self.mv2[:, 0:1], self.CA1[:], ALU.subtract, ALU.mult))
        self.drain('V')
        self.wait('V', chS, smark)
        self.op('V', lambda E: E.scalar_tensor_tensor(
            out_ap, self.tmp_t[:], self.rstd[:], self.SAt[:], ALU.mult, ALU.add), chV)
        self.drain('V')

    def _emit_ln_mod(self, x_ap, out_ap):
        smark = self._emit_ln_stats(x_ap)
        self._emit_ln_apply(x_ap, out_ap, smark)

    def _emit_bcast(self, rhs2, dst_sb, plus1=False, mk=None):
        chT, chV, chS = self.chT, self.chV, self.chS
        self.waitm('T', mk)
        self.wait('T', chV)
        self.wait('T', chS)
        for half in range(2):
            self.op('T', lambda E, hh=half: E.matmul(
                self.pG[:, hh * 512:(hh + 1) * 512], self.ones2f[:],
                rhs2[:, hh * 512:(hh + 1) * 512], start=True, stop=True),
                chT if half == 1 else None)
        self.wait('S', chT)
        self.drain('S')
        self.op('S', lambda E, p1=(1.0 if plus1 else 0.0): E.activation(
            dst_sb[:], self.pG[:, 0:1024], AF.Copy, bias=p1), chS)

    def _alloc_attn(self, sc):
        T, NTC = self.T, self.NTC
        b = lambda n, s, d=BF16: self.sbuf(n, s, d, sc)
        self.cosb = b("cosb", [128, T])
        self.sinb = b("sinb", [128, T])
        self.pswb = b("pswb", [128, 128])
        usz = max(8 * T, 14464)
        self.union = b("union", [128, usz])
        u = self.union
        self.h_fm = u[:, 0:8 * T].rearrange("p (c t) -> p c t", c=8)
        self.o_sb = u[:, 0:8 * 512].rearrange("p (h t) -> p h t", h=8)
        self.e_bf = [u[:, 4096:4608], u[:, 4608:5120]]
        self.diagb = u[:, 5120:5632]
        self.recipb = self.sbuf("recipb", [128, 512], F32, sc)
        self.wo_w = u[:, 6272:14464].rearrange("p (c d) -> p c d", c=8)
        self.q_fm = b("q_fm", [128, NH, T])
        self.k_fm = b("k_fm", [128, NH, T])
        self.v_tm = b("v_tm", [128, NTC, D])
        self.stag = b("stag", [128, NH, 512])
        if NTC * D >= 8192:
            vflat = self.v_tm[:, :, :].rearrange("p a b -> p (a b)")
            self.stag2 = vflat[:, 0:4096].rearrange("p (h t) -> p h t", h=NH)
            self.ropet = vflat[:, 4096:8192].rearrange("p (h t) -> p h t", h=NH)
        else:
            self.stag2 = b("stag2", [128, NH, 512])
            self.ropet = b("ropet", [128, NH, 512])
        self.wbuf = b("wbuf", [128, 8, 4, DH])
        self.hmod_t = self.stag[:, 0:2, :].rearrange("p a b -> p (a b)")
        self._a3_tail = {}

    def _emit_attention(self, s):
        T, NTC, NQB = self.T, self.NTC, self.NQB
        chT, chV, chS = self.chT, self.chV, self.chS
        dY, dG, dO, dX, dR = self.dY, self.dG, self.dO, self.dX, self.dR
        pA, pB, pC, pD, pEF, pG, pH = (self.pA, self.pB, self.pC, self.pD,
                                       self.pEF, self.pG, self.pH)
        mrow = lambda a, b: self.mod_scr[s, a:b].rearrange("(o d) -> o d", o=1)

        self.wait('Y', dO)
        self.wait('Y', chT)
        mk = self.dma_pool('Y', self.rhs2z[0:1, :], mrow(1024, 2048))
        self._emit_bcast(self.rhs2z, self.CA1, plus1=True, mk=mk)
        self.wait('Y', chT)
        mk = self.dma_pool('Y', self.rhs2z[0:1, :], mrow(0, 1024))
        self._emit_bcast(self.rhs2z, self.SAt, mk=mk)
        self.wait('Y', chT)
        mk = self.dma_pool('Y', self.rhs2z[0:1, :], mrow(2048, 3072))
        self._emit_bcast(self.rhs2z, self.GAt, mk=mk)

        def a1_stats(tc):
            b = tc % 2
            self.waitm('Y', self.r_xt.guard())
            self.dma('Y', self.xt[b][:], self.x_in[s, tc * 128:(tc + 1) * 128, :], dX[b])
            self.wait('V', dX[b])
            return self._emit_ln_stats(self.xt[b][:])

        smark = a1_stats(0)
        for tc in range(NTC):
            b = tc % 2
            self._emit_ln_apply(self.xt[b][:], self.hmod_t[:], smark)
            self.r_xt.done(chV)
            if tc + 1 < NTC:
                smark = a1_stats(tc + 1)
            self.wait('T', chV)
            self.waitm('T', self.r_pH.guard())
            for c in range(8):
                self.op('T', lambda E, cc=c: E.transpose(
                    pH[:, cc, :], self.hmod_t[:, cc * 128:(cc + 1) * 128],
                    self.identb[:]), chT if c == 7 else None)
            self.wait('V', chT)
            self.op('V', lambda E, t=tc: E.tensor_copy(
                self.h_fm[:, :, t * 128:(t + 1) * 128], pH[:]), chV)
            self.drain('V')
            self.r_pH.done(chV)

        NTB = T // 512
        qk_ring = Ring(2)
        swap_ring = Ring(2)
        for w_d, dst in [(self.wq_d, self.q_fm), (self.wk_d, self.k_fm)]:
            for hf in range(2):
                self.wait('G', chT)
                for h4 in range(4):
                    self.dma('G', self.wbuf[:, :, h4, :],
                             w_d[hf * 4 + h4].rearrange("(c p) d -> p c d", p=128), dG)
                self.wait('T', dG)
                self.wait('T', chV)
                for tb in range(NTB):
                    for h4 in range(4):
                        h = hf * 4 + h4
                        ps = pA if h4 % 2 == 0 else pB
                        self.waitm('T', qk_ring.guard())
                        for c in range(8):
                            self.op('T', lambda E, cc=c, hh=h4, t=tb, p=ps: E.matmul(
                                p[:], self.wbuf[:, cc, hh, :],
                                self.h_fm[:, cc, t * 512:(t + 1) * 512],
                                start=(cc == 0), stop=(cc == 7)),
                                chT if c == 7 else None)
                        self.wait('V', chT)
                        self.op('V', lambda E, hh=h, p=ps: E.tensor_copy(
                            self.stag[:, hh, :], p[:]), chV)
                        qk_ring.done(chV)
                    self._emit_rope(dst, tb, swap_ring, hf)
        v_ring = Ring(1)
        for hf in range(2):
            self.wait('G', chT)
            for h4 in range(4):
                self.dma('G', self.wbuf[:, :, h4, :],
                         self.wv_d[hf * 4 + h4].rearrange("(c p) d -> p c d", p=128), dG)
            self.wait('T', dG)
            wflat = self.wbuf[:, :, :, :].rearrange("p c h d -> p c (h d)")
            for tc in range(NTC):
                self.waitm('T', v_ring.guard())
                for c in range(8):
                    self.op('T', lambda E, cc=c, t=tc, wf=wflat: E.matmul(
                        pG[:, 0:512], self.h_fm[:, cc, t * 128:(t + 1) * 128],
                        wf[:, cc, 0:512],
                        start=(cc == 0), stop=(cc == 7)),
                        chT if c == 7 else None)
                self.wait('V', chT)
                self.op('V', lambda E, t=tc, o=hf * 512: E.tensor_copy(
                    self.v_tm[:, t, o:o + 512], pG[:, 0:512]), chV)
                self.drain('V')
                v_ring.done(chV)

        self.wait('G', chT)
        self.dma('G', self.wo_w[:, :, :],
                 self.wo_d.rearrange("(c p) d -> p c d", p=128), dG)
        self.dma('G', self.diagb[:], self.diag_d, dG)
        self.wait('T', dG)

        sc_ring = Ring(2)
        og_ring = Ring(1)
        self.wait('T', chV)
        tailst = self._a3_tail

        def sums_row(par, co=0):
            return pEF[0:1, co:512] if par == 0 else pG[0:1, 512 + co:1024]

        def emit_tail():
            t = tailst.pop('t', None)
            if t is None:
                return
            par, pO_, stop_mark, hh = t
            self.wait('V', chT, stop_mark)
            self.drain('V')
            if tailst.get('bcast') is not None:
                self.wait('V', chT, tailst['bcast'])
            self.op('V', lambda E, r=sums_row(par): E.reciprocal(
                self.rhs2z[0:1, 0:512], r), chV)
            rec_mark = chV.n
            self.wait('T', chV, rec_mark)
            if tailst.get('copy') is not None:
                self.wait('T', chV, tailst['copy'])
            if tailst.get('og') is not None:
                self.wait('T', chV, tailst['og'])
            self.op('T', lambda E: E.matmul(pG[:, 0:512], self.ones2f[:],
                                            self.rhs2z[:, 0:512],
                                            start=True, stop=True), chT)
            tailst['bcast'] = chT.n
            self.wait('V', chT)
            self.drain('V')
            self.op('V', lambda E: E.tensor_copy(self.recipb[:], pG[:, 0:512]), chV)
            tailst['copy'] = chV.n
            self.drain('V')
            self.op('V', lambda E, hh=hh, p=pO_: E.tensor_tensor(
                self.o_sb[:, hh, :], p[:], self.recipb[:], ALU.mult), chV)
            self.drain('V')
            tailst['ev%d' % par] = chV.n
            tailst['ln%d' % par] = rec_mark

        for qb in range(NQB):
            for h in range(NH):
                par = h % 2
                pO = pC if h % 2 == 0 else pD
                kcs = [(kc, 0, False) for kc in range(4 * qb)]
                kcs += [(4 * qb + d, 128 * d, True) for d in range(4)]
                if tailst.get('ln%d' % par) is not None:
                    self.wait('T', chV, tailst['ln%d' % par])
                if par == 1 and tailst.get('og') is not None:
                    self.wait('T', chV, tailst['og'])
                if tailst.get('ev%d' % par) is not None:
                    self.wait('T', chV, tailst['ev%d' % par])
                sc_marks = {}
                tt_mark = {}

                def emit_scores(j):
                    kc, coff, isdiag = kcs[j]
                    pS = pA if j % 2 == 0 else pB
                    self.waitm('T', sc_ring.guard())
                    self.op('T', lambda E, k_=kc, q_=qb, co=coff, n_=512 - coff,
                            p=pS, hh=h, sp=(not isdiag):
                            E.matmul(p[:, 0:n_],
                                     self.k_fm[:, hh, k_ * 128:(k_ + 1) * 128],
                                     self.q_fm[:, hh, q_ * 512 + co:(q_ + 1) * 512],
                                     start=True, stop=sp),
                            None if isdiag else chT)
                    if isdiag:
                        self.op('T', lambda E, n_=512 - coff, p=pS: E.matmul(
                            p[:, 0:n_], self.identb[:], self.diagb[:, 0:n_],
                            start=False, stop=True), chT)
                    sc_marks[j] = chT.n

                emit_scores(0)
                for j, (kc, coff, isdiag) in enumerate(kcs):
                    ncols = 512 - coff
                    last = (j == len(kcs) - 1)
                    pS = pA if j % 2 == 0 else pB
                    eb = self.e_bf[j % 2]
                    if j + 1 < len(kcs):
                        emit_scores(j + 1)
                    self.wait('S', chT, sc_marks[j])
                    self.op('S', lambda E, p=pS, n_=ncols, b=eb:
                            E.activation(b[:, 0:n_], p[:, 0:n_], AF.Exp,
                                         scale=0.0078125), chS)
                    self.wait('T', chS)
                    self.op('T', lambda E, k_=kc, co=coff, n_=ncols, b=eb, hh=h,
                            st=(j == 0), sp=last, p=pO:
                            E.matmul(p[:, co:512],
                                     self.v_tm[:, k_, hh * DH:(hh + 1) * DH],
                                     b[:, 0:n_], start=st, stop=sp), chT)
                    self.op('T', lambda E, co=coff, n_=ncols, b=eb,
                            st=(j == 0), sp=last, pr=par:
                            E.matmul(sums_row(pr, co), self.ones_c[:], b[:, 0:n_],
                                     start=st, stop=sp), chT)
                    sc_ring.done(chT)
                new_t = (par, pO, chT.n, h)
                emit_tail()
                tailst['t'] = new_t
                next(self._wp_feed, None)
                if h == NH - 1:
                    emit_tail()
            for qc in range(4):
                tc = qb * 4 + qc
                b = tc % 2
                self.waitm('Y', self.r_xt.guard())
                self.dma('Y', self.xt[b][:], self.x_in[s, tc * 128:(tc + 1) * 128, :], dR[b])
                self.wait('T', chV)
                self.wait('T', dG)
                self.waitm('T', og_ring.guard())
                for h in range(NH):
                    for half in range(2):
                        self.op('T', lambda E, hh=h, q_=qc, ha=half: E.matmul(
                            pG[:, ha * 512:(ha + 1) * 512],
                            self.o_sb[:, hh, q_ * 128:(q_ + 1) * 128],
                            self.wo_w[:, hh, ha * 512:(ha + 1) * 512],
                            start=(hh == 0), stop=(hh == NH - 1)),
                            chT if (h == NH - 1 and half == 1) else None)
                self.wait('V', chT)
                self.wait('V', dR[b])
                self.waitm('V', self.r_xw.guard())
                self.drain('V')
                self.wait('V', chS)
                self.op('V', lambda E: E.scalar_tensor_tensor(
                    self.tmp_t[:], pG[:, 0:1024], 1.0, self.GAt[:],
                    ALU.mult, ALU.mult))
                self.drain('V')
                self.op('V', lambda E, bb=b: E.tensor_tensor(
                    self.xw[bb][:], self.tmp_t[:], self.xt[bb][:], ALU.add), chV)
                self.drain('V')
                self.r_xt.done(chV)
                og_ring.done(chV)
                self.wait('Y', chV)
                self.dma('Y', self.xmid_scr[s, tc * 128:(tc + 1) * 128, :],
                         self.xw[b][:], self.dS[b])
                self.r_xw.done_at(self.dS[b], self.dS[b].n)
            tailst['og'] = chV.n

    def _emit_rope(self, dst, tb, swap_ring, hf):
        chT, chV = self.chT, self.chV
        h0, h1 = hf * 4, hf * 4 + 4
        cs = self.cosb[:, tb * 512:(tb + 1) * 512].unsqueeze(1).broadcast_to([128, 4, 512])
        sn = self.sinb[:, tb * 512:(tb + 1) * 512].unsqueeze(1).broadcast_to([128, 4, 512])
        d0 = dst[:, h0:h1, tb * 512:(tb + 1) * 512]
        st = self.stag[:, h0:h1, :]
        st2 = self.stag2[:, h0:h1, :]
        rt = self.ropet[:, h0:h1, :]
        for h in range(h0, h1):
            ps = self.pC if h % 2 == 0 else self.pD
            self.waitm('T', swap_ring.guard())
            self.op('T', lambda E, hh=h, p=ps: E.matmul(
                p[:], self.pswb[:], self.stag[:, hh, :], start=True, stop=True), chT)
            self.wait('V', chT)
            self.op('V', lambda E, hh=h, p=ps: E.tensor_copy(
                self.stag2[:, hh, :], p[:]), chV)
            swap_ring.done(chV)
        self.drain('V')
        self.op('V', lambda E: E.tensor_tensor(rt, st2, sn, ALU.mult))
        self.drain('V')
        self.op('V', lambda E: E.tensor_tensor(st2, st, cs, ALU.mult))
        self.drain('V')
        self.op('V', lambda E: E.tensor_tensor(d0[:], st2, rt, ALU.add), chV)
        self.drain('V')

    def _alloc_ffn(self, sc):
        b = lambda n, s, d=BF16: self.sbuf(n, s, d, sc)
        self.wgT = b("wgT", [128, 8, I_FFN])
        self.wuT = b("wuT", [128, 8, I_FFN])
        self.wdT = b("wdT", [128, 22, D])
        self.xqf = b("xqf", [128, 8, 128])
        self.xq_t = b("xq_t", [128, D])
        self.hq_tp = [[b("hq_t00", [128, 512]), b("hq_t01", [128, 512])],
                      [b("hq_t10", [128, 512]), b("hq_t11", [128, 512])]]
        self.hqf = b("hqf", [128, 8, 128])
        self.hmid = [self.sbuf("hmid0", [128, 512], BF16, sc),
                     self.sbuf("hmid1", [128, 512], BF16, sc)]
        self.h2x = self.sbuf("h2x", [128, D], BF16, sc)
        self._qdm_prev = None
        self.dM = [self.chan("dM0"), self.chan("dM1")]

    def _emit_ffn(self, s):
        NTC = self.NTC
        chT, chV, chS = self.chT, self.chV, self.chS
        dY, dO, dM = self.dY, self.dO, self.dM
        pA, pB, pC, pD, pG, pH = self.pA, self.pB, self.pC, self.pD, self.pG, self.pH
        mrow = lambda a, b: self.mod_scr[s, a:b].rearrange("(o d) -> o d", o=1)

        self.wait('Y', chT)
        mk = self.dma_pool('Y', self.rhs2z[0:1, :], mrow(4096, 5120))
        self._emit_bcast(self.rhs2z, self.CA1, plus1=True, mk=mk)
        self.wait('Y', chT)
        mk = self.dma_pool('Y', self.rhs2z[0:1, :], mrow(3072, 4096))
        self._emit_bcast(self.rhs2z, self.SAt, mk=mk)
        self.wait('Y', chT)
        mk = self.dma_pool('Y', self.rhs2z[0:1, :], mrow(5120, 6144))
        self._emit_bcast(self.rhs2z, self.GAt, mk=mk)

        NIB = (I_FFN + 511) // 512
        gu_ring = Ring(1)
        dn_ring = Ring(1)
        for mk in self.ffn_wmk:
            self.waitm('T', mk)

        def stage_b1(tc):
            b = tc % 2
            self.waitm('Y', self.r_xt.guard())
            self.dma('Y', self.xt[b][:], self.xmid_scr[s, tc * 128:(tc + 1) * 128, :], dM[b])
            self.wait('V', dM[b])
            return self._emit_ln_stats(self.xt[b][:])

        def stage_b2(tc, smark):
            b = tc % 2
            self._emit_ln_apply(self.xt[b][:], self.h2x[:], smark)
            self._emit_qd([(self.h2x[:, 0:512], self.xq_t[:, 0:512], 512),
                           (self.h2x[:, 512:1024], self.xq_t[:, 512:1024], 512)])
            self.op('V', lambda E: E.engine_nop(), chV)
            return chV.n

        xq_ready = stage_b2(0, stage_b1(0))
        b1_mark = stage_b1(1) if NTC > 1 else None
        for tc in range(NTC):
            b = tc % 2
            self.wait('T', chV, xq_ready)
            self.waitm('T', self.r_pH.guard())
            for c in range(8):
                self.op('T', lambda E, cc=c: E.transpose(
                    pH[:, cc, :], self.xq_t[:, cc * 128:(cc + 1) * 128],
                    self.identb[:]), chT if c == 7 else None)
            self.wait('V', chT)
            self.op('V', lambda E: E.tensor_copy(self.xqf[:], pH[:]), chV)
            self.drain('V')
            self.r_pH.done(chV)
            xqf_ready = chV.n
            if tc + 1 < NTC:
                xq_ready = stage_b2(tc + 1, b1_mark)
            pairs = [[ib for ib in (ib0, ib0 + 1) if ib < NIB]
                     for ib0 in range(0, NIB, 2)]

            def emit_gu_silu(k):
                ibs = pairs[k]
                self.wait('T', chV, xqf_ready)
                self.waitm('T', gu_ring.guard())
                gms = {}
                for c in range(8):
                    for ib in ibs:
                        io = ib * 512
                        iw = min(512, I_FFN - io)
                        pGt = pA if ib % 2 == 0 else pB
                        pUt = pC if ib % 2 == 0 else pD
                        self.op('T', lambda E, cc=c, o=io, w=iw, p=pGt: E.matmul(
                            p[:, 0:w], self.xqf[:, cc, :], self.wgT[:, cc, o:o + w],
                            start=(cc == 0), stop=(cc == 7)),
                            chT if c == 7 else None)
                        if c == 7:
                            gms[ib] = chT.n
                        self.op('T', lambda E, cc=c, o=io, w=iw, p=pUt: E.matmul(
                            p[:, 0:w], self.xqf[:, cc, :], self.wuT[:, cc, o:o + w],
                            start=(cc == 0), stop=(cc == 7)),
                            chT if c == 7 else None)
                gu_mark = chT.n
                self.drain('S')
                if self._qdm_prev is not None:
                    self.wait('S', chV, self._qdm_prev)
                for i, ib in enumerate(ibs):
                    iw = min(512, I_FFN - ib * 512)
                    pGt = pA if ib % 2 == 0 else pB
                    self.wait('S', chT, gms[ib])
                    self.op('S', lambda E, w=iw, p=pGt, i=i: E.activation(
                        self.hmid[i][:, 0:w], p[:, 0:w], AF.Silu),
                        chS if ib == ibs[-1] else None)
                return gu_mark

            def emit_vblock(k, gu_mark):
                ibs = pairs[k]
                hqt = self.hq_tp[k % 2]
                self.wait('V', chS)
                self.wait('V', chT, gu_mark)
                self.drain('V')
                for i, ib in enumerate(ibs):
                    iw = min(512, I_FFN - ib * 512)
                    pUt = pC if ib % 2 == 0 else pD
                    self.op('V', lambda E, w=iw, p=pUt, i=i: E.tensor_tensor(
                        self.hmid[i][:, 0:w], self.hmid[i][:, 0:w], p[:, 0:w],
                        ALU.mult), chV if ib == ibs[-1] else None)
                gu_ring.done(chV)
                self._emit_qd([(self.hmid[i][:, 0:min(512, I_FFN - ib * 512)],
                                hqt[i][:, 0:min(512, I_FFN - ib * 512)],
                                min(512, I_FFN - ib * 512))
                               for i, ib in enumerate(ibs)])
                self.op('V', lambda E: E.engine_nop(), chV)
                return chV.n

            def emit_htdn(k, qdm):
                ibs = pairs[k]
                hqt = self.hq_tp[k % 2]
                nst_tot = sum(min(512, I_FFN - ib * 512) // 128 for ib in ibs)
                self.wait('T', chV, qdm)
                self.waitm('T', self.r_pH.guard())
                kk = 0
                for i, ib in enumerate(ibs):
                    nst = min(512, I_FFN - ib * 512) // 128
                    for c in range(nst):
                        self.op('T', lambda E, cc=c, k2=kk, hq=hqt[i]: E.transpose(
                            pH[:, k2, :], hq[:, cc * 128:(cc + 1) * 128],
                            self.identb[:]), chT if kk == nst_tot - 1 else None)
                        kk += 1
                self.wait('V', chT)
                self.op('V', lambda E, n=nst_tot: E.tensor_copy(
                    self.hqf[:, 0:n, :], pH[:, 0:n, :]), chV)
                self.drain('V')
                self.r_pH.done(chV)
                if ibs[0] == 0:
                    self.waitm('T', dn_ring.guard())
                self.wait('T', chV)
                kk = 0
                for i, ib in enumerate(ibs):
                    nst = min(512, I_FFN - ib * 512) // 128
                    for c in range(nst):
                        icg = ib * 4 + c
                        lastd = (ib == NIB - 1 and c == nst - 1)
                        for half in range(2):
                            self.op('T', lambda E, k2=kk, ig=icg, ha=half,
                                    st=(ib == 0 and c == 0), sp=lastd: E.matmul(
                                pG[:, ha * 512:(ha + 1) * 512], self.hqf[:, k2, :],
                                self.wdT[:, ig, ha * 512:(ha + 1) * 512],
                                start=st, stop=sp),
                                chT if (kk == nst_tot - 1 and half == 1) else None)
                        kk += 1

            prev = None
            for k in range(len(pairs)):
                gu_mark = emit_gu_silu(k)
                if prev is not None:
                    emit_htdn(*prev)
                qdm = emit_vblock(k, gu_mark)
                self._qdm_prev = self.qd_srcfree
                prev = (k, qdm)
            emit_htdn(*prev)
            self.wait('V', chT)
            self.waitm('V', self.r_xw.guard())
            self.drain('V')
            self.op('V', lambda E: E.tensor_tensor(
                self.tmp_t[:], pG[:, 0:1024], self.GAt[:], ALU.mult))
            self.drain('V')
            self.op('V', lambda E, bb=b: E.tensor_tensor(
                self.xw[bb][:], self.tmp_t[:], self.xt[bb][:], ALU.add), chV)
            self.drain('V')
            self.r_xt.done(chV)
            dn_ring.done(chV)
            self.wait('Y', chV)
            self.dma('Y', self.out_d[s, tc * 128:(tc + 1) * 128, :], self.xw[b][:],
                     self.dS[b])
            self.r_xw.done_at(self.dS[b], self.dS[b].n)
            if tc + 2 < NTC:
                b1_mark = stage_b1(tc + 2)


def _host_consts(T):
    i = np.arange(32, dtype=np.float64)
    t = np.arange(T, dtype=np.float64)
    freqs = np.power(10000.0, -i / 32.0)
    rad = t[None, :] * freqs[:, None]
    cos_t = np.cos(rad)
    sin_t = np.sin(rad)
    C = np.ones((128, T))
    S = np.zeros((128, T))
    C[0:32] = cos_t
    C[32:64] = cos_t
    S[0:32] = -sin_t
    S[32:64] = sin_t
    jj = np.arange(128)
    ee = np.arange(512)
    diag01 = np.where(jj[:, None] > ee[None, :], -1e30, 0.0).astype(np.float32)
    ident = np.eye(128, dtype=np.float32)
    sw = np.concatenate([np.arange(32, 64), np.arange(0, 32), np.arange(64, 128)])
    pswap = np.zeros((128, 128), np.float32)
    pswap[np.arange(128), sw] = 1.0
    return C.astype(np.float32), S.astype(np.float32), diag01, ident, pswap


def make_in_maps(L0_attn_x, L0_attn_wq, L0_attn_wk, L0_attn_wv, L0_attn_wo,
                 L0_ffn_w_gate, L0_ffn_w_up, L0_ffn_w_down, L0_c, L0_wmod,
                 T=2048, NSEQ=4, ncores=NCORES):
    ctab, stab, diag01, ident, pswap = _host_consts(T)
    asc = lambda a: np.ascontiguousarray(np.asarray(a, dtype=np.float32))
    maps = []
    for core in range(ncores):
        s0 = core * NSEQ
        maps.append({
            "x": asc(L0_attn_x[s0:s0 + NSEQ, :T]),
            "wq": asc(L0_attn_wq), "wk": asc(L0_attn_wk), "wv": asc(L0_attn_wv),
            "wo": asc(L0_attn_wo),
            "w_gate": asc(L0_ffn_w_gate), "w_up": asc(L0_ffn_w_up),
            "w_down": asc(L0_ffn_w_down),
            "wmod": asc(L0_wmod),
            "cvec": asc(L0_c[s0:s0 + NSEQ]),
            "cos_t": ctab, "sin_t": stab, "diag01": diag01, "ident": ident,
            "pswap": pswap,
        })
    return maps


def kernel(L0_attn_x, L0_attn_wq, L0_attn_wk, L0_attn_wv, L0_attn_wo,
           L0_ffn_w_gate, L0_ffn_w_up, L0_ffn_w_down, L0_c, L0_wmod):
    b = Builder(T=2048, NSEQ=4)
    in_maps = make_in_maps(L0_attn_x, L0_attn_wq, L0_attn_wk, L0_attn_wv,
                           L0_attn_wo, L0_ffn_w_gate, L0_ffn_w_up,
                           L0_ffn_w_down, L0_c, L0_wmod)
    res = run_bass_kernel_spmd(b.nc, in_maps, core_ids=list(range(NCORES)))
    return np.concatenate([r["out"] for r in res.results], axis=0)
